# version 19
# baseline (speedup 1.0000x reference)
from contextlib import ExitStack
import math
import numpy as np
import ml_dtypes
import concourse.bass as bass
import concourse.mybir as mybir
from concourse.bass_utils import run_bass_kernel_spmd

F32 = mybir.dt.float32
BF16 = mybir.dt.bfloat16
AF = mybir.ActivationFunctionType
OP = mybir.AluOpType
AX = mybir.AxisListType.X

P = 128
D = 4096
FC = D // P
NS = 64
RANK = 32
NL = 16
EPS = 1e-5
N_CORES = 8


class Prog:
    ENGS = ["sync", "gpsimd", "vector", "scalar", "tensor"]

    def __init__(self, nc, ctx):
        self.nc = nc
        self.ctx = ctx
        self.ops = {en: [] for en in self.ENGS}
        self.esem = {en: ctx.enter_context(nc.semaphore("tok_" + en))
                     for en in ["vector", "scalar", "tensor"]}
        self.en_cnt = {en: 0 for en in ["vector", "scalar", "tensor"]}
        self.dsems = {}

    def emit(self, en, fn):
        sem = self.esem[en]
        if en in ("vector", "scalar") and self.en_cnt[en] > 0:
            v = self.en_cnt[en]
            self.ops[en].append(lambda e, sem=sem, v=v: e.wait_ge(sem, v))
        self.en_cnt[en] += 1

        def wrapped(e, fn=fn, sem=sem):
            fn(e).then_inc(sem, 1)
        self.ops[en].append(wrapped)
        return (sem, self.en_cnt[en])

    def now(self, en):
        return (self.esem[en], self.en_cnt[en])

    def dma(self, en, stream, fn):
        if stream not in self.dsems:
            self.dsems[stream] = [self.ctx.enter_context(self.nc.semaphore("dma_" + stream)), 0]
        rec = self.dsems[stream]
        sem = rec[0]
        rec[1] += 16

        def wrapped(e, fn=fn, sem=sem):
            fn(e).then_inc(sem, 16)
        self.ops[en].append(wrapped)
        return (sem, rec[1])

    def waits(self, en, toks):
        best = {}
        for t in toks:
            if t is None:
                continue
            sem, v = t
            k = id(sem)
            if k not in best or best[k][1] < v:
                best[k] = (sem, v)
        for sem, v in best.values():
            self.ops[en].append(lambda e, sem=sem, v=v: e.wait_ge(sem, v))

    def build(self, block):
        for en in self.ENGS:
            section = getattr(block, en)
            ops = self.ops[en]

            def make(ops):
                def _(e):
                    for fn in ops:
                        fn(e)
                return _
            section(make(ops))


_CONSTS = None


def host_constants():
    global _CONSTS
    if _CONSTS is not None:
        return _CONSTS
    L = D
    lv = (np.arange(L, dtype=np.float32) / np.float32(L)).astype(np.complex64)
    Omega = np.exp(np.complex64(-2j * np.pi) * lv)
    w = ((np.complex64(1) - Omega) / (np.complex64(1) + Omega)).astype(np.complex64)
    c2 = (np.complex64(2) / (np.complex64(1) + Omega)).astype(np.complex64)

    l1 = np.arange(64)[:, None]
    r1 = np.arange(64)[None, :]
    th = 2.0 * np.pi * l1 * (r1 + 1) / 64.0
    Fre = np.cos(th).astype(np.float32)
    Fim = (-np.sin(th)).astype(np.float32)
    Fimneg = np.sin(th).astype(np.float32)
    r1c = np.arange(64)[:, None]
    l2 = np.arange(64)[None, :]
    thtw = 2.0 * np.pi * l2 * (r1c + 1) / 4096.0
    TWre = np.cos(thtw).astype(np.float32)
    TWim = (-np.sin(thtw)).astype(np.float32)
    l2c = np.arange(64)[:, None]
    r2 = np.arange(64)[None, :]
    thg = 2.0 * np.pi * l2c * r2 / 64.0
    Gre = (np.cos(thg) / 4096.0).astype(np.float32)
    Gimneg = (np.sin(thg) / 4096.0).astype(np.float32)

    _CONSTS = {
        "identb": np.eye(P, dtype=ml_dtypes.bfloat16),
        "identf": np.eye(P, dtype=np.float32),
        "ones2b": np.ones((2, P), dtype=ml_dtypes.bfloat16),
        "ones1f": np.ones((1, P), dtype=np.float32),
        "ones64f": np.ones((1, NS), dtype=np.float32),
        "whre": np.ascontiguousarray(w.real.reshape(1, L)).astype(np.float32),
        "whim": np.ascontiguousarray(w.imag.reshape(1, L)).astype(np.float32),
        "c2re": np.ascontiguousarray(c2.real.reshape(P, 32)).astype(np.float32),
        "c2im": np.ascontiguousarray(c2.imag.reshape(P, 32)).astype(np.float32),
        "Fre": Fre, "Fim": Fim, "Fimneg": Fimneg,
        "TWre": TWre, "TWim": TWim,
        "Gre": Gre, "Gimneg": Gimneg,
    }
    return _CONSTS


def build_nc(t_tiles=8):
    nc = bass.Bass("TRN2", target_bir_lowering=False)
    TT = t_tiles
    TOK = TT * P

    u = nc.dram_tensor("u", [TOK, D], F32, kind="ExternalInput").ap()
    lam_ri = nc.dram_tensor("lam_ri", [NS, 2], F32, kind="ExternalInput").ap()
    p_ri = nc.dram_tensor("p_ri", [NS, 2], F32, kind="ExternalInput").ap()
    q_ri = nc.dram_tensor("q_ri", [NS, 2], F32, kind="ExternalInput").ap()
    vc_ri = nc.dram_tensor("vc_ri", [NS, NS * 2], F32, kind="ExternalInput").ap()
    ct_ri = nc.dram_tensor("ct_ri", [NS, 2], F32, kind="ExternalInput").ap()
    bp_in = nc.dram_tensor("bp", [1, NS], F32, kind="ExternalInput").ap()
    dsc = nc.dram_tensor("dsc", [1, 1], F32, kind="ExternalInput").ap()
    lstep = nc.dram_tensor("lstep", [1, 1], F32, kind="ExternalInput").ap()
    wk = nc.dram_tensor("wk", [D, D], F32, kind="ExternalInput").ap()
    bias_in = nc.dram_tensor("bias", [1, D], F32, kind="ExternalInput").ap()
    lora_a = nc.dram_tensor("lora_a", [NL * RANK, D], F32, kind="ExternalInput").ap()
    lora_b = nc.dram_tensor("lora_b", [NL, D * RANK], F32, kind="ExternalInput").ap()
    lora_as = nc.dram_tensor("lora_as", [RANK, D], F32, kind="ExternalInput").ap()
    lora_bs = nc.dram_tensor("lora_bs", [1, D * RANK], F32, kind="ExternalInput").ap()
    gateq = nc.dram_tensor("gateq", [P, 4], F32, kind="ExternalInput").ap()
    gater = nc.dram_tensor("gater", [P, NL], F32, kind="ExternalInput").ap()

    C = host_constants()
    cin = {}
    for name, arr in C.items():
        dt = BF16 if arr.dtype == ml_dtypes.bfloat16 else F32
        cin[name] = nc.dram_tensor(name, list(arr.shape), dt, kind="ExternalInput").ap()

    y_out = nc.dram_tensor("y", [TOK, D], F32, kind="ExternalOutput").ap()

    x1_scr = nc.dram_tensor("x1_scr", [TOK, D], F32).ap()
    f_scr = nc.dram_tensor("f_scr", [TOK, D], F32).ap()
    krev_t = nc.dram_tensor("krev_scr", [1, 5120], BF16)
    ar_scr = nc.dram_tensor("ar_scr", [2, D], F32).ap()
    k_scr = nc.dram_tensor("k_scr", [16, D], F32).ap()
    bias_scr = nc.dram_tensor("bias_scr", [2, D], BF16).ap()
    bpad_scr_t = nc.dram_tensor("bpad_scr", [D, P], BF16)
    bpad_scr = bpad_scr_t.ap()

    with ExitStack() as ctx:
        e = ctx.enter_context

        xb = [e(nc.sbuf_tensor("xb0", [P, D], F32)), e(nc.sbuf_tensor("xb1", [P, D], F32))]
        ya = e(nc.sbuf_tensor("ya", [P, D], F32))
        n1b = e(nc.sbuf_tensor("n1b", [P, D], BF16))
        n1c = e(nc.sbuf_tensor("n1c", [P, D], BF16))
        xfmt = e(nc.sbuf_tensor("xfmt", [P, D], BF16))
        trv = e(nc.sbuf_tensor("trv", [P, D], BF16))
        tra = e(nc.sbuf_tensor("tra", [P, D], BF16))
        tband = e(nc.sbuf_tensor("tband", [P, 4480], BF16))
        n2fm = e(nc.sbuf_tensor("n2fm", [P, FC * TOK], BF16))
        if FC * TOK >= 4480:
            tbandr = n2fm
        else:
            tbandr = e(nc.sbuf_tensor("tbandr", [P, 4480], BF16))
        astkT = e(nc.sbuf_tensor("astkT", [P, FC * NS], BF16))
        bstkT = e(nc.sbuf_tensor("bstkT", [2 * RANK, D], BF16))
        axb = e(nc.sbuf_tensor("axb", [2 * RANK, TOK], BF16))
        bhl = e(nc.sbuf_tensor("bhl", [2, D], BF16))
        wbuf = [e(nc.sbuf_tensor(f"wbuf{i}", [P, 512], BF16)) for i in range(4)]
        fstage = [e(nc.sbuf_tensor(f"fstage{i}", [P, 512], F32)) for i in range(4)]
        p3b = [e(nc.sbuf_tensor(f"p3b{i}", [P, 512], F32)) for i in range(4)]

        idsb = e(nc.sbuf_tensor("idsb", [P, P], BF16))
        idsf = e(nc.sbuf_tensor("idsf", [P, P], F32))
        ones2bs = e(nc.sbuf_tensor("ones2bs", [2, P], BF16))
        ones1s = e(nc.sbuf_tensor("ones1s", [1, P], F32))
        ones64s = e(nc.sbuf_tensor("ones64s", [1, NS], F32))
        lam_s = e(nc.sbuf_tensor("lam_s", [NS, 2], F32))
        pq_s = e(nc.sbuf_tensor("pq_s", [NS, 4], F32))
        ct_s = e(nc.sbuf_tensor("ct_s", [NS, 2], F32))
        vc_s = None
        bp_s = e(nc.sbuf_tensor("bp_s", [1, NS], F32))
        bpb = None
        bvec = e(nc.sbuf_tensor("bvec", [NS, 2], F32))
        vmat = e(nc.sbuf_tensor("vmat", [NS, 8], F32))
        tmp641 = None
        sstep = e(nc.sbuf_tensor("sstep", [1, 1], F32))
        lnbias = e(nc.sbuf_tensor("lnbias", [NS, 1], F32))
        stepb = e(nc.sbuf_tensor("stepb", [NS, 1], F32))
        dbc = e(nc.sbuf_tensor("dbc", [P, 1], F32))
        dtile = e(nc.sbuf_tensor("dtile", [1, 1], F32))
        gq_s = e(nc.sbuf_tensor("gq_s", [P, 4], F32))
        gr_s = e(nc.sbuf_tensor("gr_s", [P, NL], F32))

        kt = [p3b[0][:, 32 * i:32 * i + 32] for i in range(16)]
        cc = [p3b[1][:, 32 * i:32 * i + 32] for i in range(6)]
        arre = p3b[2][:, 0:32]
        arim = p3b[2][:, 32:64]
        c2re_s = p3b[3][:, 0:32]
        c2im_s = p3b[3][:, 32:64]

        _p0s = ([p3b[1][0:NS, 192 + 64 * i:256 + 64 * i] for i in range(5)]
                + [p3b[2][0:NS, 64 + 64 * i:128 + 64 * i] for i in range(7)]
                + [p3b[3][0:NS, 64 + 64 * i:128 + 64 * i] for i in range(7)])
        fft_s = {nm: _p0s[i] for i, nm in enumerate(
            ["Fre", "Fim", "Fimneg", "TWre", "TWim", "Gre", "Gimneg"])}
        arm_re = _p0s[7]
        arm_im = _p0s[8]
        Are = _p0s[9]
        Aim = _p0s[10]
        Bre = _p0s[11]
        Bim = _p0s[12]
        Btre = _p0s[13]
        Btim = _p0s[14]
        bpb = _p0s[15]
        tmp641 = _p0s[16]
        vc_s = p3b[3][0:NS, 384:512]
        kmatb = e(nc.sbuf_tensor("kmatb", [NS, NS], BF16))
        kmatbT = e(nc.sbuf_tensor("kmatbT", [NS, NS], BF16))
        wh_s = [fstage[0][0:1, :], fstage[1][0:1, :]]
        b128 = e(nc.sbuf_tensor("b128", [P, 32], F32))
        bhi = e(nc.sbuf_tensor("bhi", [P, 32], BF16))
        bhif = e(nc.sbuf_tensor("bhif", [P, 32], F32))
        blo = e(nc.sbuf_tensor("blo", [P, 32], BF16))

        def stat_site(tag):
            return {nm: e(nc.sbuf_tensor(f"{tag}_{nm}", [P, 1], F32))
                    for nm in ["s1", "q1", "mean", "var", "rstd", "nbias"]}
        st1 = stat_site("st1")
        st1b = stat_site("st1b")
        sta = stat_site("sta")
        st2 = stat_site("st2")
        stf = stat_site("stf")
        nall = e(nc.sbuf_tensor("nall", [P, 8], F32))
        yacols = e(nc.sbuf_tensor("yacols", [P, 8], F32))
        yaqcols = e(nc.sbuf_tensor("yaqcols", [P, 8], F32))
        fsums = [e(nc.sbuf_tensor(f"fsums{t}", [P, 8], F32)) for t in range(TT)]
        fsqs = [e(nc.sbuf_tensor(f"fsqs{t}", [P, 8], F32)) for t in range(TT)]

        pr = Prog(nc, ctx)
        E = pr.emit
        W = pr.waits
        DMA = pr.dma

        ps_cms = []

        def psum_open(name, shape, dt):
            cm = nc.psum_tensor(name, shape, dt)
            ps_cms.append(cm)
            return cm.__enter__()

        putil = psum_open("putil", [P, 512], F32)
        pconv = [psum_open(f"pconv{i}", [P, 512], F32) for i in range(3)]
        ptrb = [psum_open(f"ptrb{i}", [P, 1024], BF16) for i in range(2)]
        pax = psum_open("pax", [2 * RANK, TOK], F32)

        tc = {}
        const_loads = [
            ("identb", idsb), ("identf", idsf), ("ones2b", ones2bs),
            ("ones1f", ones1s), ("ones64f", ones64s),
            ("c2re", c2re_s), ("c2im", c2im_s),
        ]
        for name, dst in const_loads:
            tc[name] = DMA("sync", "consts", lambda s, d=dst, src=cin[name]: s.dma_start(out=d[:, :], in_=src))
        for nm in fft_s:
            tc[nm] = DMA("sync", "consts", lambda s, d=fft_s[nm], src=cin[nm]: s.dma_start(out=d[:, :], in_=src))
        tc["gateq"] = DMA("sync", "consts", lambda s: s.dma_start(out=gq_s[:, :], in_=gateq))
        tc["gater"] = DMA("sync", "consts", lambda s: s.dma_start(out=gr_s[:, :], in_=gater))

        tsm = []
        for src, dst in [(lam_ri, lam_s[:, :]), (ct_ri, ct_s[:, :]), (vc_ri, vc_s[:, :]),
                         (bp_in, bp_s[:, :]), (dsc, dtile[:, :]), (lstep, sstep[:, :])]:
            tsm.append(DMA("sync", "smalls", lambda s, src=src, dst=dst: s.dma_start(out=dst, in_=src)))
        tsm.append(DMA("sync", "smalls", lambda s: s.dma_start(out=pq_s[:, 0:2], in_=p_ri)))
        tsm.append(DMA("sync", "smalls", lambda s: s.dma_start(out=pq_s[:, 2:4], in_=q_ri)))
        tsm.append(DMA("sync", "smalls", lambda s: s.dma_start(
            out=b128[:, :], in_=bias_in.rearrange("1 (p c) -> p c", p=P))))
        t_call = (pr.dsems["consts"][0], pr.dsems["consts"][1])
        t_sall = (pr.dsems["smalls"][0], pr.dsems["smalls"][1])
        for _k in list(tc):
            tc[_k] = t_call
        tsm = [t_sall]

        W("scalar", tsm)
        t_exp = E("scalar", lambda s: s.activation(out=sstep[:, :], in_=sstep[:, :], func=AF.Exp, scale=-1.0))
        W("vector", [t_exp])
        t_2os = E("vector", lambda v: v.tensor_scalar(out=sstep[:, :], in0=sstep[:, :], scalar1=2.0, scalar2=None, op0=OP.mult))
        W("tensor", [tc["ones64f"], t_2os])
        t_mm1 = E("tensor", lambda t: t.matmul(out=putil[0:NS, 0:1], lhsT=ones64s[:, :], rhs=sstep[:, :], start=True, stop=True))
        W("vector", [t_mm1])
        t_stepb = E("vector", lambda v: v.tensor_copy(out=stepb[:, :], in_=putil[0:NS, 0:1]))

        W("tensor", [tc["ones1f"], t_stepb] + tsm)
        t_mm2 = E("tensor", lambda t: t.matmul(out=putil[:, 1:2], lhsT=ones1s[:, :], rhs=dtile[:, :], start=True, stop=True))
        W("vector", [t_mm2])
        t_dbc = E("vector", lambda v: v.tensor_copy(out=dbc[:, :], in_=putil[:, 1:2]))

        W("tensor", [t_dbc])
        t_mm3 = E("tensor", lambda t: t.matmul(out=putil[0:NS, 0:NS], lhsT=ones64s[:, :], rhs=bp_s[:, :], start=True, stop=True))
        W("vector", [t_mm3])
        t_bpb = E("vector", lambda v: v.tensor_copy(out=bpb[:, :], in_=putil[0:NS, 0:NS]))

        W("vector", [t_bpb])
        t1 = E("vector", lambda v: v.tensor_tensor(out=tmp641[:, :], in0=bass.AP(p3b[3], 384, [[512, NS], [2, NS]]), in1=bpb[:, :], op=OP.mult))
        W("vector", [t1])
        t_bre = E("vector", lambda v: v.reduce_sum(out=bvec[:, 0:1], in_=tmp641[:, :], axis=AX))
        t2 = E("vector", lambda v: v.tensor_tensor(out=tmp641[:, :], in0=bass.AP(p3b[3], 385, [[512, NS], [2, NS]]), in1=bpb[:, :], op=OP.mult))
        W("vector", [t2])
        t_bim = E("vector", lambda v: v.reduce_sum(out=bvec[:, 1:2], in_=tmp641[:, :], axis=AX))
        W("vector", [t_bre, t_bim])

        def cmul_conj(col, are, aim, bre, bim):
            ta = E("vector", lambda v: v.tensor_tensor(out=vmat[:, 2 * col:2 * col + 1], in0=are, in1=bre, op=OP.mult))
            tb = E("vector", lambda v: v.tensor_tensor(out=tmp641[:, 0:1], in0=aim, in1=bim, op=OP.mult))
            W("vector", [ta, tb])
            E("vector", lambda v: v.tensor_tensor(out=vmat[:, 2 * col:2 * col + 1], in0=vmat[:, 2 * col:2 * col + 1], in1=tmp641[:, 0:1], op=OP.add))
            tcsh = E("vector", lambda v: v.tensor_tensor(out=vmat[:, 2 * col + 1:2 * col + 2], in0=are, in1=bim, op=OP.mult))
            td = E("vector", lambda v: v.tensor_tensor(out=tmp641[:, 1:2], in0=aim, in1=bre, op=OP.mult))
            W("vector", [tcsh, td])
            return E("vector", lambda v: v.tensor_tensor(out=vmat[:, 2 * col + 1:2 * col + 2], in0=vmat[:, 2 * col + 1:2 * col + 2], in1=tmp641[:, 1:2], op=OP.subtract))

        t_v0 = cmul_conj(0, ct_s[:, 0:1], ct_s[:, 1:2], bvec[:, 0:1], bvec[:, 1:2])
        t_v1 = cmul_conj(1, ct_s[:, 0:1], ct_s[:, 1:2], pq_s[:, 0:1], pq_s[:, 1:2])
        t_v2 = cmul_conj(2, pq_s[:, 2:3], pq_s[:, 3:4], bvec[:, 0:1], bvec[:, 1:2])
        t_v3 = cmul_conj(3, pq_s[:, 2:3], pq_s[:, 3:4], pq_s[:, 0:1], pq_s[:, 1:2])
        t_vv = [t_v0, t_v1, t_v2, t_v3]

        gb_toks = []
        gb_mms = []
        prev_sc = None
        pot = [putil, pconv[0], pconv[1]]
        kwh = 0
        for half, dst in [(0, xb[0]), (1, xb[1])]:
            for cchunk in range(8):
                src_c = cin["whre"] if half == 0 else cin["whim"]
                if kwh >= 2:
                    W("sync", [gb_mms[kwh - 2]])
                t_wh = DMA("sync", f"wh{kwh % 2}", lambda s, src_c=src_c, cchunk=cchunk, kwh=kwh: s.dma_start(
                    out=wh_s[kwh % 2][:, :], in_=src_c[0:1, 512 * cchunk:512 * cchunk + 512]))
                toks = [t_wh, t_stepb, t_bpb]
                if kwh >= 3:
                    toks.append(gb_toks[kwh - 3])
                W("tensor", toks)
                tmm = E("tensor", lambda t, kwh=kwh: t.matmul(
                    out=pot[kwh % 3][0:NS, :], lhsT=ones64s[:, :],
                    rhs=wh_s[kwh % 2][:, :], start=True, stop=True))
                gb_mms.append(tmm)
                W("scalar", [tmm])
                prev_sc = E("scalar", lambda s, dst=dst, cchunk=cchunk, kwh=kwh: s.activation(
                    out=dst[0:NS, 512 * cchunk:512 * cchunk + 512], in_=pot[kwh % 3][0:NS, :],
                    func=AF.Identity, scale=stepb[:, :]))
                gb_toks.append(prev_sc)
                kwh += 1

        W("vector", gb_toks + t_vv)
        t_dre = E("vector", lambda v: v.tensor_scalar(out=xb[0][0:NS, :], in0=xb[0][0:NS, :], scalar1=lam_s[:, 0:1], scalar2=None, op0=OP.subtract))
        t_dim = E("vector", lambda v: v.tensor_scalar(out=xb[1][0:NS, :], in0=xb[1][0:NS, :], scalar1=lam_s[:, 1:2], scalar2=None, op0=OP.subtract))
        W("scalar", [t_dre])
        t_s1 = E("scalar", lambda s: s.activation(out=ya[0:NS, :], in_=xb[0][0:NS, :], func=AF.Square))
        W("scalar", [t_dim])
        t_s2 = E("scalar", lambda s: s.activation(out=xb[1][0:NS, :], in_=xb[1][0:NS, :], func=AF.Square))
        W("vector", [t_s1, t_s2])
        t_s12 = E("vector", lambda v: v.tensor_tensor(out=ya[0:NS, :], in0=ya[0:NS, :], in1=xb[1][0:NS, :], op=OP.add))
        W("scalar", [t_s12])
        t_ln = E("scalar", lambda s: s.activation(out=ya[0:NS, :], in_=ya[0:NS, :], func=AF.Ln, scale=float(2.0 ** -32)))
        W("scalar", [t_ln])
        t_lb = E("vector", lambda v: v.memset(lnbias[:, :], float(-32.0 * math.log(2.0))))
        W("scalar", [t_lb])
        t_rec = E("scalar", lambda s: s.activation(out=ya[0:NS, :], in_=ya[0:NS, :], func=AF.Exp, scale=-1.0, bias=lnbias[:, :]))
        W("vector", [t_rec])
        t_rdre = E("vector", lambda v: v.tensor_tensor(out=xb[0][0:NS, :], in0=xb[0][0:NS, :], in1=ya[0:NS, :], op=OP.mult))
        prev_sc2 = t_s12
        for cchunk in range(8):
            if kwh >= 2:
                W("sync", [gb_mms[kwh - 2]])
            t_wh = DMA("sync", f"wh{kwh % 2}", lambda s, cchunk=cchunk, kwh=kwh: s.dma_start(
                out=wh_s[kwh % 2][:, :], in_=cin["whim"][0:1, 512 * cchunk:512 * cchunk + 512]))
            W("tensor", [t_wh, gb_toks[kwh - 3], t_s12 if cchunk == 0 else prev_sc2])
            tmm = E("tensor", lambda t, kwh=kwh: t.matmul(
                out=pot[kwh % 3][0:NS, :], lhsT=ones64s[:, :],
                rhs=wh_s[kwh % 2][:, :], start=True, stop=True))
            gb_mms.append(tmm)
            W("scalar", [tmm])
            prev_sc2 = E("scalar", lambda s, cchunk=cchunk, kwh=kwh: s.activation(
                out=xb[1][0:NS, 512 * cchunk:512 * cchunk + 512], in_=pot[kwh % 3][0:NS, :],
                func=AF.Identity, scale=stepb[:, :]))
            gb_toks.append(prev_sc2)
            kwh += 1
        W("vector", [prev_sc2])
        t_dim2 = E("vector", lambda v: v.tensor_scalar(out=xb[1][0:NS, :], in0=xb[1][0:NS, :], scalar1=lam_s[:, 1:2], scalar2=None, op0=OP.subtract))
        t_rdim = E("vector", lambda v: v.scalar_tensor_tensor(out=xb[1][0:NS, :], in0=xb[1][0:NS, :], scalar=-1.0, op0=OP.mult, in1=ya[0:NS, :], op1=OP.mult))

        prev_ev = None
        ks_evs = []
        kk = 0
        for src, outrow, tok in [(xb[0], 0, t_rdre), (xb[1], 32, t_rdim)]:
            for cchunk in range(8):
                toks = [tok, t_rec, t_rdim]
                if kk >= 3:
                    toks.append(ks_evs[kk - 3])
                else:
                    toks.append(gb_toks[-1])
                W("tensor", toks)
                tmm = E("tensor", lambda t, src=src, cchunk=cchunk, kk=kk: t.matmul(
                    out=pot[kk % 3][0:8, :], lhsT=vmat[:, :], rhs=src[0:NS, 512 * cchunk:512 * cchunk + 512],
                    start=True, stop=True))
                W("vector", [tmm])
                prev_ev = E("vector", lambda v, outrow=outrow, cchunk=cchunk, kk=kk: v.tensor_copy(
                    out=ya[outrow:outrow + 8, 512 * cchunk:512 * cchunk + 512], in_=pot[kk % 3][0:8, :]))
                ks_evs.append(prev_ev)
                kk += 1
        t_ksum_done = prev_ev
        W("gpsimd", [t_ksum_done])
        t_ks1 = DMA("gpsimd", "kscr", lambda g: g.dma_start(out=k_scr[0:8, :], in_=ya[0:8, :]))
        t_ks2 = DMA("gpsimd", "kscr", lambda g: g.dma_start(out=k_scr[8:16, :], in_=ya[32:40, :]))
        W("gpsimd", [t_ks1, t_ks2])
        t_ktl = []
        for i in range(16):
            t_ktl.append(DMA("gpsimd", "kscr", lambda g, i=i: g.dma_start(
                out=kt[i][:, :], in_=k_scr[i:i + 1, :].rearrange("1 (p c) -> p c", p=P))))

        W("vector", t_ktl + [tc["c2re"], tc["c2im"]])
        TTE = lambda fn: E("vector", fn)
        for k in range(4):
            a = TTE(lambda v, k=k: v.tensor_tensor(out=kt[2 * k][:, :], in0=kt[2 * k][:, :], in1=kt[8 + 2 * k + 1][:, :], op=OP.subtract))
            b = TTE(lambda v, k=k: v.tensor_tensor(out=kt[8 + 2 * k][:, :], in0=kt[8 + 2 * k][:, :], in1=kt[2 * k + 1][:, :], op=OP.add))
            W("vector", [a, b])
        t = TTE(lambda v: v.tensor_scalar(out=kt[6][:, :], in0=kt[6][:, :], scalar1=1.0, scalar2=None, op0=OP.add))
        W("vector", [t])
        t = TTE(lambda v: v.tensor_tensor(out=cc[0][:, :], in0=kt[6][:, :], in1=kt[6][:, :], op=OP.mult))
        t2b = TTE(lambda v: v.tensor_tensor(out=cc[1][:, :], in0=kt[14][:, :], in1=kt[14][:, :], op=OP.mult))
        W("vector", [t, t2b])
        t = TTE(lambda v: v.tensor_tensor(out=cc[0][:, :], in0=cc[0][:, :], in1=cc[1][:, :], op=OP.add))
        W("vector", [t])
        t_recd = TTE(lambda v: v.reciprocal(out=cc[0][:, :], in_=cc[0][:, :]))
        a1 = TTE(lambda v: v.tensor_tensor(out=cc[1][:, :], in0=kt[2][:, :], in1=kt[4][:, :], op=OP.mult))
        a2 = TTE(lambda v: v.tensor_tensor(out=cc[2][:, :], in0=kt[10][:, :], in1=kt[12][:, :], op=OP.mult))
        W("vector", [a1, a2])
        t_numre = TTE(lambda v: v.tensor_tensor(out=cc[1][:, :], in0=cc[1][:, :], in1=cc[2][:, :], op=OP.subtract))
        a3 = TTE(lambda v: v.tensor_tensor(out=cc[2][:, :], in0=kt[2][:, :], in1=kt[12][:, :], op=OP.mult))
        a4 = TTE(lambda v: v.tensor_tensor(out=cc[3][:, :], in0=kt[10][:, :], in1=kt[4][:, :], op=OP.mult))
        W("vector", [a3, a4])
        t_numim = TTE(lambda v: v.tensor_tensor(out=cc[2][:, :], in0=cc[2][:, :], in1=cc[3][:, :], op=OP.add))
        W("vector", [t_numre, t_numim, t_recd])
        b1 = TTE(lambda v: v.tensor_tensor(out=cc[3][:, :], in0=cc[1][:, :], in1=kt[6][:, :], op=OP.mult))
        b2 = TTE(lambda v: v.tensor_tensor(out=cc[4][:, :], in0=cc[2][:, :], in1=kt[14][:, :], op=OP.mult))
        W("vector", [b1, b2])
        t = TTE(lambda v: v.tensor_tensor(out=cc[3][:, :], in0=cc[3][:, :], in1=cc[4][:, :], op=OP.add))
        W("vector", [t])
        t_rre = TTE(lambda v: v.tensor_tensor(out=cc[3][:, :], in0=cc[3][:, :], in1=cc[0][:, :], op=OP.mult))
        b3 = TTE(lambda v: v.tensor_tensor(out=cc[4][:, :], in0=cc[2][:, :], in1=kt[6][:, :], op=OP.mult))
        b4 = TTE(lambda v: v.tensor_tensor(out=cc[5][:, :], in0=cc[1][:, :], in1=kt[14][:, :], op=OP.mult))
        W("vector", [b3, b4])
        t = TTE(lambda v: v.tensor_tensor(out=cc[4][:, :], in0=cc[4][:, :], in1=cc[5][:, :], op=OP.subtract))
        W("vector", [t])
        t_rim = TTE(lambda v: v.tensor_tensor(out=cc[4][:, :], in0=cc[4][:, :], in1=cc[0][:, :], op=OP.mult))
        W("vector", [t_rre, t_rim])
        t_dfre = TTE(lambda v: v.tensor_tensor(out=cc[1][:, :], in0=kt[0][:, :], in1=cc[3][:, :], op=OP.subtract))
        t_dfim = TTE(lambda v: v.tensor_tensor(out=cc[2][:, :], in0=kt[8][:, :], in1=cc[4][:, :], op=OP.subtract))
        W("vector", [t_dfre, t_dfim])
        m1 = TTE(lambda v: v.tensor_tensor(out=arre[:, :], in0=c2re_s[:, :], in1=cc[1][:, :], op=OP.mult))
        m2 = TTE(lambda v: v.tensor_tensor(out=cc[5][:, :], in0=c2im_s[:, :], in1=cc[2][:, :], op=OP.mult))
        W("vector", [m1, m2])
        t_arre = TTE(lambda v: v.tensor_tensor(out=arre[:, :], in0=arre[:, :], in1=cc[5][:, :], op=OP.subtract))
        m3 = TTE(lambda v: v.tensor_tensor(out=arim[:, :], in0=c2re_s[:, :], in1=cc[2][:, :], op=OP.mult))
        m4 = TTE(lambda v: v.tensor_tensor(out=cc[5][:, :], in0=c2im_s[:, :], in1=cc[1][:, :], op=OP.mult))
        W("vector", [m3, m4])
        t_arim = TTE(lambda v: v.tensor_tensor(out=arim[:, :], in0=arim[:, :], in1=cc[5][:, :], op=OP.add))

        W("gpsimd", [t_arre, t_arim])
        t_ar1 = DMA("gpsimd", "kscr", lambda g: g.dma_start(
            out=ar_scr[0:1, :].rearrange("1 (p c) -> p c", p=P), in_=arre[:, :]))
        t_ar2 = DMA("gpsimd", "kscr", lambda g: g.dma_start(
            out=ar_scr[1:2, :].rearrange("1 (p c) -> p c", p=P), in_=arim[:, :]))
        W("gpsimd", [t_ar1, t_ar2])
        t_arm1 = DMA("gpsimd", "kscr", lambda g: g.dma_start(
            out=arm_re[:, :], in_=ar_scr[0:1, :].rearrange("1 (p c) -> p c", p=NS)))
        t_arm2 = DMA("gpsimd", "kscr", lambda g: g.dma_start(
            out=arm_im[:, :], in_=ar_scr[1:2, :].rearrange("1 (p c) -> p c", p=NS)))

        W("tensor", [t_arm1, t_arm2, t_ksum_done] + [tc[nm] for nm in fft_s])
        tm1 = E("tensor", lambda t: t.matmul(out=putil[0:NS, 0:NS], lhsT=fft_s["Fre"][:, :], rhs=arm_re[:, :], start=True, stop=False))
        tm2 = E("tensor", lambda t: t.matmul(out=putil[0:NS, 0:NS], lhsT=fft_s["Fimneg"][:, :], rhs=arm_im[:, :], start=False, stop=True))
        W("vector", [tm2])
        t_are = E("vector", lambda v: v.tensor_copy(out=Are[:, :], in_=putil[0:NS, 0:NS]))
        W("tensor", [t_are])
        tm3 = E("tensor", lambda t: t.matmul(out=putil[0:NS, 0:NS], lhsT=fft_s["Fim"][:, :], rhs=arm_re[:, :], start=True, stop=False))
        tm4 = E("tensor", lambda t: t.matmul(out=putil[0:NS, 0:NS], lhsT=fft_s["Fre"][:, :], rhs=arm_im[:, :], start=False, stop=True))
        W("vector", [tm4])
        t_aim = E("vector", lambda v: v.tensor_copy(out=Aim[:, :], in_=putil[0:NS, 0:NS]))
        W("vector", [t_are, t_aim])
        w1 = E("vector", lambda v: v.tensor_tensor(out=Bre[:, :], in0=Are[:, :], in1=fft_s["TWre"][:, :], op=OP.mult))
        w2 = E("vector", lambda v: v.tensor_tensor(out=tmp641[:, :], in0=Aim[:, :], in1=fft_s["TWim"][:, :], op=OP.mult))
        W("vector", [w1, w2])
        t_bre2 = E("vector", lambda v: v.tensor_tensor(out=Bre[:, :], in0=Bre[:, :], in1=tmp641[:, :], op=OP.subtract))
        w3 = E("vector", lambda v: v.tensor_tensor(out=Bim[:, :], in0=Are[:, :], in1=fft_s["TWim"][:, :], op=OP.mult))
        w4 = E("vector", lambda v: v.tensor_tensor(out=tmp641[:, :], in0=Aim[:, :], in1=fft_s["TWre"][:, :], op=OP.mult))
        W("vector", [w3, w4])
        t_bim2 = E("vector", lambda v: v.tensor_tensor(out=Bim[:, :], in0=Bim[:, :], in1=tmp641[:, :], op=OP.add))
        W("tensor", [t_bre2, tc["identf"]])
        tt1 = E("tensor", lambda t: t.transpose(out=putil[0:NS, 0:NS], in_=Bre[:, :], identity=idsf[0:NS, 0:NS]))
        W("vector", [tt1])
        t_btre = E("vector", lambda v: v.tensor_copy(out=Btre[:, :], in_=putil[0:NS, 0:NS]))
        W("tensor", [t_bim2, t_btre])
        tt2 = E("tensor", lambda t: t.transpose(out=putil[0:NS, 0:NS], in_=Bim[:, :], identity=idsf[0:NS, 0:NS]))
        W("vector", [tt2])
        t_btim = E("vector", lambda v: v.tensor_copy(out=Btim[:, :], in_=putil[0:NS, 0:NS]))
        W("tensor", [t_btim])
        tm5 = E("tensor", lambda t: t.matmul(out=putil[0:NS, 0:NS], lhsT=Btre[:, :], rhs=fft_s["Gre"][:, :], start=True, stop=False))
        tm6 = E("tensor", lambda t: t.matmul(out=putil[0:NS, 0:NS], lhsT=Btim[:, :], rhs=fft_s["Gimneg"][:, :], start=False, stop=True))
        W("vector", [tm6])
        t_kmb = E("vector", lambda v: v.tensor_copy(out=kmatb[:, :], in_=putil[0:NS, 0:NS]))
        W("tensor", [t_kmb, tc["identb"]])
        tt3 = E("tensor", lambda t: t.transpose(out=ptrb[0][0:NS, 0:NS], in_=kmatb[:, :], identity=idsb[0:NS, 0:NS]))
        W("vector", [tt3])
        t_kmbT = E("vector", lambda v: v.tensor_copy(out=kmatbT[:, :], in_=ptrb[0][0:NS, 0:NS]))
        ztile = tband
        t_zt = E("vector", lambda v: v.memset(ztile[0:1, 0:512], 0.0))
        W("gpsimd", [t_kmbT, t_zt])
        t_kv1 = DMA("gpsimd", "kscr", lambda g: g.dma_start(out=bass.AP(krev_t, 512, [[NS, NS], [1, NS]]), in_=kmatbT[:, :]))
        t_kv2 = DMA("gpsimd", "kscr", lambda g: g.dma_start(out=bass.AP(krev_t, 0, [[512, 1], [1, 512]]), in_=ztile[0:1, 0:512]))
        t_kv3 = DMA("gpsimd", "kscr", lambda g: g.dma_start(out=bass.AP(krev_t, 4608, [[512, 1], [1, 512]]), in_=ztile[0:1, 0:512]))
        W("gpsimd", [t_kv1, t_kv2, t_kv3])
        _rpitch = FC * TOK if FC * TOK >= 4480 else 4480
        t_tbr = DMA("gpsimd", "kscr", lambda g: g.dma_start(
            out=bass.AP(tbandr, 0, [[_rpitch, P], [1, 4480]]), in_=bass.AP(krev_t, 512, [[1, P], [1, 4480]])))
        W("vector", [t_tbr])
        t_tband = E("vector", lambda v: v.tensor_copy(
            out=tband[:, :], in_=bass.AP(tbandr, 4479, [[_rpitch, P], [-1, 4480]])))

        abuf = [xfmt, trv]
        t_acc = None
        t_lda_prev = []
        t_accs = []
        for k in range(4):
            toks = [t_ks1, t_ks2]
            if k >= 2:
                W("gpsimd", [t_accs[k - 2]])
            t_ld = DMA("gpsimd", f"lora{k % 2}", lambda g, k=k, b=abuf[k % 2]: g.dma_start(
                out=b[:, :], in_=lora_a[128 * k:128 * k + 128, :]))
            W("vector", [t_ld, tc["gateq"]] + toks + ([t_acc] if t_acc else []))
            if k == 0:
                t_acc = E("vector", lambda v, b=abuf[0]: v.tensor_scalar(
                    out=ya[:, :], in0=b[:, :], scalar1=gq_s[:, 0:1], scalar2=None, op0=OP.mult))
            else:
                t_acc = E("vector", lambda v, k=k, b=abuf[k % 2]: v.scalar_tensor_tensor(
                    out=ya[:, :], in0=b[:, :], scalar=gq_s[:, k:k + 1], op0=OP.mult, in1=ya[:, :], op1=OP.add))
            t_accs.append(t_acc)
            t_lda_prev.append(t_ld)
        W("vector", [t_acc])
        t_cp1 = E("vector", lambda v: v.tensor_copy(out=xb[1][0:NS, :], in_=ya[NS:P, :]))
        t_cb1 = E("vector", lambda v: v.tensor_tensor(out=ya[0:NS, :], in0=ya[0:NS, :], in1=xb[1][0:NS, :], op=OP.add))
        t_cp2 = E("vector", lambda v: v.tensor_copy(out=xb[1][0:RANK, :], in_=ya[RANK:NS, :]))
        t_cb2 = E("vector", lambda v: v.tensor_tensor(out=ya[0:RANK, :], in0=ya[0:RANK, :], in1=xb[1][0:RANK, :], op=OP.add))
        W("scalar", [t_cb2])
        t_acast = E("scalar", lambda s: s.activation(out=trv[0:RANK, :], in_=ya[0:RANK, :], func=AF.Identity))
        W("gpsimd", t_lda_prev + [t_acc])
        t_ash = DMA("gpsimd", "lash", lambda g: g.dma_start(out=trv[RANK:NS, :], in_=lora_as))
        t_aev_prev = None
        t_atr_last = None
        for r in range(4):
            toks = [t_acast, t_ash, tc["identb"], t_tband, t_kmbT]
            if t_aev_prev is not None:
                toks.append(t_aev_prev)
            W("tensor", toks)
            for j in range(8):
                c = 8 * r + j
                t_atr_last = E("tensor", lambda t, c=c, r=r, j=j: t.transpose(
                    out=ptrb[r % 2][:, 64 * j:64 * j + 64], in_=trv[0:NS, 128 * c:128 * c + 128],
                    identity=idsb[0:NS, 0:NS]))
            W("vector", [t_atr_last])
            t_aev_prev = E("vector", lambda v, r=r: v.tensor_copy(
                out=astkT[:, 512 * r:512 * r + 512], in_=ptrb[r % 2][:, 0:512]))
        t_astk = t_aev_prev

        bbuf = [n1b[:, 0:1024], n1b[:, 1024:2048]]
        t_bacc = None
        t_ldb_prev = []
        t_baccs = []
        for n in range(NL):
            if n >= 2:
                W("gpsimd", [t_baccs[n - 2]])
            t_ld = DMA("gpsimd", f"lorb{n % 2}", lambda g, n=n, b=bbuf[n % 2]: g.dma_start(
                out=b, in_=lora_b[n:n + 1, :].rearrange("1 (p c) -> p c", p=P)))
            W("vector", [t_ld, tc["gater"], t_rdre, t_ksum_done] + ([t_bacc] if t_bacc else []))
            if n == 0:
                t_bacc = E("vector", lambda v, b=bbuf[0]: v.tensor_scalar(
                    out=xb[0][:, 0:1024], in0=b, scalar1=gr_s[:, 0:1], scalar2=None, op0=OP.mult))
            else:
                t_bacc = E("vector", lambda v, n=n, b=bbuf[n % 2]: v.scalar_tensor_tensor(
                    out=xb[0][:, 0:1024], in0=b, scalar=gr_s[:, n:n + 1], op0=OP.mult, in1=xb[0][:, 0:1024], op1=OP.add))
            t_baccs.append(t_bacc)
            t_ldb_prev.append(t_ld)
        W("scalar", [t_bacc])
        t_bbcast = E("scalar", lambda s: s.activation(out=n1b[:, 2048:3072], in_=xb[0][:, 0:1024], func=AF.Identity))
        t_zf = E("vector", lambda v: v.memset(tra[:, :], 0.0))
        W("gpsimd", [t_zf])
        t_zfd = DMA("gpsimd", "lzf", lambda g: g.dma_start(
            out=bass.AP(bpad_scr_t, 0, [[4096, P], [1, 4096]]), in_=tra[:, :]))
        W("gpsimd", [t_bbcast, t_zfd])
        t_bw = DMA("gpsimd", "lmisc", lambda g: g.dma_start(
            out=bass.AP(bpad_scr_t, 0, [[RANK * P, P], [P, RANK], [1, RANK]]), in_=n1b[:, 2048:3072]))
        W("sync", [t_bw, t_atr_last])
        t_bx = DMA("sync", "lora2", lambda s: s.dma_start(out=trv[:, :], in_=bpad_scr, transpose=True))
        W("vector", [t_bx])
        t_bst1 = E("vector", lambda v: v.tensor_copy(out=bstkT[0:RANK, :], in_=trv[0:RANK, :]))
        W("gpsimd", [t_bacc])
        t_ldbs = DMA("gpsimd", "lbs", lambda g: g.dma_start(
            out=n1b[:, 0:1024], in_=lora_bs[0:1, :].rearrange("1 (p c) -> p c", p=P)))
        W("gpsimd", [t_ldbs, t_bst1])
        t_bw2 = DMA("gpsimd", "lmisc", lambda g: g.dma_start(
            out=bass.AP(bpad_scr_t, 0, [[RANK * P, P], [P, RANK], [1, RANK]]), in_=n1b[:, 0:1024]))
        W("sync", [t_bw2])
        t_bx2 = DMA("sync", "lora2", lambda s: s.dma_start(out=trv[:, :], in_=bpad_scr, transpose=True))
        W("vector", [t_bx2])
        t_bst2 = E("vector", lambda v: v.tensor_copy(out=bstkT[RANK:2 * RANK, :], in_=trv[0:RANK, :]))

        W("scalar", tsm + [t_bbcast])
        t_bh = E("scalar", lambda s: s.activation(out=bhi[:, :], in_=b128[:, :], func=AF.Identity))
        W("vector", [t_bh])
        t_bhf = E("vector", lambda v: v.tensor_copy(out=bhif[:, :], in_=bhi[:, :]))
        W("vector", [t_bhf])
        t_bl = E("vector", lambda v: v.tensor_tensor(out=blo[:, :], in0=b128[:, :], in1=bhif[:, :], op=OP.subtract))
        W("gpsimd", [t_bh, t_bl])
        t_bs1 = DMA("gpsimd", "lbias", lambda g: g.dma_start(
            out=bias_scr[0:1, :].rearrange("1 (p c) -> p c", p=P), in_=bhi[:, :]))
        t_bs2 = DMA("gpsimd", "lbias", lambda g: g.dma_start(
            out=bias_scr[1:2, :].rearrange("1 (p c) -> p c", p=P), in_=blo[:, :]))
        W("sync", [t_bs1, t_bs2])
        t_bhl = DMA("sync", "lora2", lambda s: s.dma_start(out=bhl[:, :], in_=bias_scr))

        def ln_finalize(st, n_elems, t_s1tok, t_q1tok, a_guard=None):
            guards = [st.get("_ra"), a_guard]
            W("vector", [t_s1tok, t_q1tok] + [g for g in guards if g is not None])
            a = E("vector", lambda v: v.tensor_scalar(out=st["mean"][:, :], in0=st["s1"][:, :], scalar1=1.0 / n_elems, scalar2=None, op0=OP.mult))
            b = E("vector", lambda v: v.tensor_scalar(out=st["var"][:, :], in0=st["q1"][:, :], scalar1=1.0 / n_elems, scalar2=EPS, op0=OP.mult, op1=OP.add))
            W("vector", [a, b])
            c = E("vector", lambda v: v.tensor_scalar(out=st["rstd"][:, :], in0=st["mean"][:, :], scalar1=st["mean"][:, :], scalar2=-1.0, op0=OP.mult, op1=OP.mult))
            W("vector", [c])
            d = E("vector", lambda v: v.tensor_tensor(out=st["var"][:, :], in0=st["var"][:, :], in1=st["rstd"][:, :], op=OP.add))
            W("vector", [d])
            f = E("vector", lambda v: v.reciprocal(out=st["rstd"][:, :], in_=st["var"][:, :]))
            W("scalar", [f])
            g = E("scalar", lambda s: s.activation(out=st["rstd"][:, :], in_=st["rstd"][:, :], func=AF.Sqrt))
            st["_ra"] = g
            W("vector", [g])
            h = E("vector", lambda v: v.tensor_scalar(out=st["nbias"][:, :], in0=st["mean"][:, :], scalar1=st["rstd"][:, :], scalar2=-1.0, op0=OP.mult, op1=OP.mult))
            return g, h

        t_x1store = [None] * TT
        t_uload = [None] * TT
        trv_guard = [t_bst1, t_bst2]
        tra_guard = [t_zfd]
        n1b_guard = [t_bw2]
        xfmt_guard = [t_astk]
        ya_guard = [t_acc, t_cb1, t_cb2, t_acast]
        xb0_guard = [t_bbcast, t_bacc]
        xb1_guard = [t_rdim, t_cb1, t_cb2] + gb_toks

        n1bufs = [n1b, n1c]
        st1p = [st1, st1b]
        t_n1_tok = [None] * TT
        t_n2_tok = [None] * TT
        t_rs2_tok = [None] * TT
        t_n2tr = [None] * TT
        ctr_ev = [None] * TT
        conv_ev_last = [None] * TT
        ptrb_last = [None, None]
        sqa_tok = [None] * TT

        def emit_load_ln1(t_i):
            xbt = xb[t_i % 2]
            st = st1p[t_i % 2]
            nb = n1bufs[t_i % 2]
            war = [t_x1store[t_i - 2], t_n2_tok[t_i - 2], t_rs2_tok[t_i - 2]] if t_i >= 2 else (
                [t_ksum_done, t_bacc, t_bbcast] + xb0_guard if t_i == 0
                else [t_rdim, prev_ev] + xb1_guard + [t_ksum_done])
            W("sync", war)
            t_uload[t_i] = DMA("sync", f"uin{t_i % 2}", lambda s, t_i=t_i, xbt=xbt: s.dma_start(
                out=xbt[:, :], in_=u[P * t_i:P * t_i + P, :]))
            W("vector", [t_uload[t_i]] + (trv_guard if t_i == 0 else []))
            t_rs = E("vector", lambda v, xbt=xbt, st=st: v.tensor_scalar(
                out=trv[:, :], in0=xbt[:, :], scalar1=1.0, scalar2=None, op0=OP.mult, op1=OP.add,
                accum_out=st["s1"][:, :]))
            W("scalar", [t_uload[t_i]] + (tra_guard if t_i == 0 else []))
            t_sq = E("scalar", lambda s, xbt=xbt, st=st: s.activation(
                out=tra[:, :], in_=xbt[:, :], func=AF.Square, accum_out=st["q1"][:, :]))
            t_r1, t_nb1 = ln_finalize(st, D, t_rs, t_sq,
                                      a_guard=t_n1_tok[t_i - 2] if t_i >= 2 else None)
            toks = [t_nb1]
            if t_i >= 2:
                toks += [t_n2tr[t_i - 2], conv_ev_last[t_i - 2]]
            if t_i == 0:
                toks += [t_ldbs] + n1b_guard
            W("scalar", toks)
            t_n1_tok[t_i] = E("scalar", lambda s, xbt=xbt, st=st, nb=nb: s.activation(
                out=nb[:, :], in_=xbt[:, :], func=AF.Identity, bias=st["nbias"][:, :], scale=st["rstd"][:, :]))

        def emit_convtransp(t_i):
            nb = n1bufs[t_i % 2]
            evs = []
            for r in range(4):
                b = r % 2
                toks = [t_n1_tok[t_i], ptrb_last[b]]
                if t_i == 0 and r < 2:
                    toks += [t_astk, t_kmbT] + xfmt_guard
                W("tensor", toks)
                tlast = None
                for j in range(8):
                    c = 8 * r + j
                    tlast = E("tensor", lambda t, c=c, r=r, j=j, nb=nb: t.transpose(
                        out=ptrb[r % 2][:, 128 * j:128 * j + 128], in_=nb[:, 128 * c:128 * c + 128],
                        identity=idsb[:, :]))
                W("vector", [tlast] + (xfmt_guard if t_i == 0 and r == 0 else []))
                ev = E("vector", lambda v, r=r: v.tensor_copy(
                    out=xfmt[:, 1024 * r:1024 * r + 1024], in_=ptrb[r % 2][:, :]))
                ptrb_last[b] = ev
                evs.append(ev)
            ctr_ev[t_i] = evs

        def emit_conv(t_i):
            nb = n1bufs[t_i % 2]
            conv_ev = [None] * 8
            for g in range(8):
                toks = [ctr_ev[t_i][3], t_tband]
                if g >= 3:
                    toks.append(conv_ev[g - 3])
                elif t_i > 0:
                    toks.append(conv_ev_last[t_i - 1])
                else:
                    toks += [t_ksum_done, gb_toks[-1]]
                W("tensor", toks)
                tg = None
                for ij in range(4 * g + 4):
                    j0 = 512 * g - 128 * ij + 384
                    tg = E("tensor", lambda t, g=g, ij=ij, j0=j0: t.matmul(
                        out=pconv[g % 3][:, :], lhsT=xfmt[:, 128 * ij:128 * ij + 128],
                        rhs=tband[:, j0:j0 + 512], start=(ij == 0), stop=(ij == 4 * g + 3)))
                gw = []
                if g == 0:
                    gw = ([sqa_tok[t_i - 1]] if t_i >= 1 else []) + (ya_guard if t_i == 0 else [])
                W("vector", [tg] + gw)
                conv_ev[g] = E("vector", lambda v, g=g, nb=nb: v.scalar_tensor_tensor(
                    out=ya[:, 512 * g:512 * g + 512], in0=nb[:, 512 * g:512 * g + 512],
                    scalar=dbc[:, :], op0=OP.mult, in1=pconv[g % 3][:, :], op1=OP.add,
                    accum_out=yacols[:, g:g + 1]))
                W("scalar", [conv_ev[g]])
                E("scalar", lambda s, g=g: s.activation(
                    out=tra[:, 512 * g:512 * g + 512], in_=ya[:, 512 * g:512 * g + 512],
                    func=AF.Square, accum_out=yaqcols[:, g:g + 1]))
            conv_ev_last[t_i] = conv_ev[7]
            sqa_tok[t_i] = pr.now("scalar")

        def emit_post1(t_i):
            xbt = xb[t_i % 2]
            W("vector", [conv_ev_last[t_i]])
            t_ys = E("vector", lambda v: v.tensor_scalar(
                out=trv[:, 0:8], in0=yacols[:, :], scalar1=1.0, scalar2=None, op0=OP.mult, op1=OP.add,
                accum_out=sta["s1"][:, :]))
            W("vector", [sqa_tok[t_i]])
            t_yq = E("vector", lambda v: v.tensor_scalar(
                out=trv[:, 8:16], in0=yaqcols[:, :], scalar1=1.0, scalar2=None, op0=OP.mult, op1=OP.add,
                accum_out=sta["q1"][:, :]))
            t_ra, t_nba = ln_finalize(sta, D, t_ys, t_yq)
            W("vector", [t_ra, t_nba])
            E("vector", lambda v, t_i=t_i: v.tensor_copy(out=nall[:, t_i:t_i + 1], in_=sta["nbias"][:, :]))
            t_x1 = E("vector", lambda v, xbt=xbt: v.scalar_tensor_tensor(
                out=xbt[:, :], in0=ya[:, :], scalar=sta["rstd"][:, :], op0=OP.mult,
                in1=xbt[:, :], op1=OP.add))
            W("sync", [t_x1])
            t_x1store[t_i] = DMA("sync", "x1out", lambda s, t_i=t_i, xbt=xbt: s.dma_start(
                out=x1_scr[P * t_i:P * t_i + P, :], in_=xbt[:, :]))
            W("vector", [t_x1])
            t_rs2 = E("vector", lambda v, xbt=xbt: v.tensor_scalar(
                out=trv[:, :], in0=xbt[:, :], scalar1=1.0, scalar2=None, op0=OP.mult, op1=OP.add,
                accum_out=st2["s1"][:, :]))
            t_rs2_tok[t_i] = t_rs2
            W("scalar", [t_x1])
            t_sq2 = E("scalar", lambda s, xbt=xbt: s.activation(
                out=tra[:, :], in_=xbt[:, :], func=AF.Square, accum_out=st2["q1"][:, :]))
            t_r2, t_nb2 = ln_finalize(st2, D, t_rs2, t_sq2,
                                      a_guard=t_n2_tok[t_i - 1] if t_i >= 1 else None)
            W("scalar", [t_nb2, conv_ev_last[t_i]])
            t_n2_tok[t_i] = E("scalar", lambda s, xbt=xbt, nb=n1bufs[t_i % 2]: s.activation(
                out=nb[:, :], in_=xbt[:, :], func=AF.Identity, bias=st2["nbias"][:, :], scale=st2["rstd"][:, :]))

        def emit_post2(t_i):
            nb = n1bufs[t_i % 2]
            last = None
            for r in range(4):
                b = r % 2
                W("tensor", [t_n2_tok[t_i], ptrb_last[b]])
                tlast = None
                for j in range(8):
                    c = 8 * r + j
                    tlast = E("tensor", lambda t, c=c, r=r, j=j, nb=nb: t.transpose(
                        out=ptrb[r % 2][:, 128 * j:128 * j + 128], in_=nb[:, 128 * c:128 * c + 128],
                        identity=idsb[:, :]))
                W("vector", [tlast])
                t_ev2 = E("vector", lambda v, r=r, t_i=t_i: v.tensor_copy(
                    out=bass.AP(n2fm, TOK * 8 * r + P * t_i, [[FC * TOK, P], [TOK, 8], [1, P]]),
                    in_=ptrb[r % 2][:, :]))
                ptrb_last[b] = t_ev2
                W("tensor", [t_ev2, t_astk])
                for j in range(8):
                    c = 8 * r + j
                    last = E("tensor", lambda t, c=c, t_i=t_i: t.matmul(
                        out=pax[:, P * t_i:P * t_i + P], lhsT=astkT[:, NS * c:NS * c + NS],
                        rhs=bass.AP(n2fm, TOK * c + P * t_i, [[FC * TOK, P], [1, P]]),
                        start=(c == 0), stop=(c == FC - 1)))
            t_n2tr[t_i] = last

        for t_i in range(TT):
            emit_load_ln1(t_i)
            emit_convtransp(t_i)
            emit_conv(t_i)
            emit_post1(t_i)
            emit_post2(t_i)
        last_axmm = t_n2tr[TT - 1]
        last_convevac = conv_ev_last[TT - 1]
        last_ptrb_evac = ptrb_last[1]

        W("vector", [last_axmm])
        t_axb = E("vector", lambda v: v.tensor_copy(out=axb[:, :], in_=pax[:, :]))

        for cm in reversed(ps_cms):
            cm.__exit__(None, None, None)
        ps_cms2 = []

        def psum_open2(name, shape, dt):
            cm = nc.psum_tensor(name, shape, dt)
            ps_cms2.append(cm)
            return cm.__enter__()

        pffn = [psum_open2(f"pffn{i}", [P, 512], F32) for i in range(8)]
        ps1_clear = [t_axb, last_convevac, ptrb_last[0], ptrb_last[1], t_kmbT]

        t_wload = {}
        t_fev = {}
        t_fsq = {}
        t_fst = {}
        t_ffmm = {}
        t_p1v = pr.now("vector")
        t_p1a = pr.now("scalar")
        p3_stt = {}
        p3_yst = {}
        t_x1pf = [None] * TT
        W("sync", [t_p1v, t_p1a, t_x1store[TT - 1]] + [t_n2_tok[TT - 1], t_rs2_tok[TT - 1] if TT >= 2 else None])
        for tpf in range(min(2, TT)):
            t_x1pf[tpf] = DMA("sync", f"x1i{tpf % 2}", lambda s, tpf=tpf: s.dma_start(
                out=xb[tpf % 2][:, :], in_=x1_scr[P * tpf:P * tpf + P, :]))
        kglob = 0
        for oc in range(8):
            W("tensor", [t_bhl, t_axb, tc["ones2b"]] + ps1_clear)
            for t2 in range(TT):
                if oc > 0:
                    W("tensor", [t_fev[(oc - 1, t2)]])
                E("tensor", lambda t, t2=t2, oc=oc: t.matmul(
                    out=pffn[t2][:, :], lhsT=ones2bs[:, :], rhs=bhl[:, 512 * oc:512 * oc + 512],
                    start=True, stop=False))
            for f in range(FC):
                kg = kglob
                war = [t_ffmm[kg - 4]] if kg >= 4 else []
                W("gpsimd", war)
                t_wload[kg] = DMA("gpsimd", f"wload{kg % 4}", lambda g, f=f, oc=oc, kg=kg: g.dma_start(
                    out=wbuf[kg % 4][:, :], in_=wk[128 * f:128 * f + 128, 512 * oc:512 * oc + 512]))
                W("tensor", [t_wload[kg]])
                tlast = None
                for t2 in range(TT):
                    tlast = E("tensor", lambda t, f=f, t2=t2, kg=kg: t.matmul(
                        out=pffn[t2][:, :], lhsT=bass.AP(n2fm, TOK * f + P * t2, [[FC * TOK, P], [1, P]]),
                        rhs=wbuf[kg % 4][:, :], start=False, stop=False))
                t_ffmm[kg] = tlast
                kglob += 1
            for t2 in range(TT):
                t_dl = E("tensor", lambda t, t2=t2, oc=oc: t.matmul(
                    out=pffn[t2][:, :], lhsT=axb[:, P * t2:P * t2 + P], rhs=bstkT[:, 512 * oc:512 * oc + 512],
                    start=False, stop=True))
                idx = oc * TT + t2
                sbuf_i = idx % 4
                war = []
                if idx >= 4:
                    rprev = (idx - 4) % TT
                    rec = pr.dsems[f"frow{rprev}"]
                    war = [(rec[0], rec[1]), t_fsq[idx - 4]]
                W("vector", [t_dl] + war)
                t_fev[(oc, t2)] = E("vector", lambda v, t2=t2, oc=oc, sbuf_i=sbuf_i: v.tensor_scalar(
                    out=fstage[sbuf_i][:, :], in0=pffn[t2][:, :], scalar1=1.0, scalar2=None,
                    op0=OP.mult, op1=OP.add, accum_out=fsums[t2][:, oc:oc + 1]))
                W("scalar", [t_fev[(oc, t2)]])
                t_fsq[idx] = E("scalar", lambda s, sbuf_i=sbuf_i, t2=t2, oc=oc: s.activation(
                    out=tra[:, 512 * sbuf_i:512 * sbuf_i + 512], in_=fstage[sbuf_i][:, :],
                    func=AF.Square, accum_out=fsqs[t2][:, oc:oc + 1]))
                W("sync", [t_fev[(oc, t2)]])
                t_fst[idx] = DMA("sync", f"frow{t2}", lambda s, t2=t2, oc=oc, sbuf_i=sbuf_i: s.dma_start(
                    out=f_scr[P * t2:P * t2 + P, 512 * oc:512 * oc + 512], in_=fstage[sbuf_i][:, :]))
                if oc == 7:
                    t_i = t2
                    W("vector", [t_fev[(7, t_i)]])
                    t_fs = E("vector", lambda v, t_i=t_i: v.tensor_scalar(
                        out=trv[:, 0:8], in0=fsums[t_i][:, :], scalar1=1.0, scalar2=None, op0=OP.mult, op1=OP.add,
                        accum_out=stf["s1"][:, :]))
                    W("vector", [t_fsq[7 * TT + t_i]])
                    t_fq = E("vector", lambda v, t_i=t_i: v.tensor_scalar(
                        out=trv[:, 8:16], in0=fsqs[t_i][:, :], scalar1=1.0, scalar2=None, op0=OP.mult, op1=OP.add,
                        accum_out=stf["q1"][:, :]))
                    t_rf, t_nbf = ln_finalize(stf, D, t_fs, t_fq, a_guard=stf.get("_lnf"))
                    rec_f = pr.dsems[f"frow{t_i}"]
                    t_frow_all = (rec_f[0], rec_f[1])
                    for pc in range(8):
                        i3 = t_i * 8 + pc
                        sl = i3 % 4
                        fbuf = p3b[sl]
                        war = [t_frow_all]
                        if i3 >= 4:
                            war.append(p3_stt[i3 - 4])
                        else:
                            war += [t_p1v, t_p1a]
                        W("sync", war)
                        t_fl = DMA("sync", f"fi{sl}", lambda s, t_i=t_i, pc=pc, fbuf=fbuf: s.dma_start(
                            out=fbuf[:, :], in_=f_scr[P * t_i:P * t_i + P, 512 * pc:512 * pc + 512]))
                        W("scalar", [t_nbf, t_fl])
                        t_lnf = E("scalar", lambda s, fbuf=fbuf: s.activation(
                            out=fbuf[:, :], in_=fbuf[:, :], func=AF.Identity, bias=stf["nbias"][:, :], scale=stf["rstd"][:, :]))
                        stf["_lnf"] = t_lnf
                        yw = [t_x1pf[t_i], t_lnf]
                        if i3 >= 8:
                            rec_y = pr.dsems[f"yo{(i3 - 8) % 4}"]
                            yw.append((rec_y[0], rec_y[1]))
                        if i3 == 0:
                            yw.append(t_p1v)
                        W("vector", yw)
                        p3_stt[i3] = E("vector", lambda v, t_i=t_i, pc=pc, fbuf=fbuf: v.scalar_tensor_tensor(
                            out=ya[:, 512 * pc:512 * pc + 512], in0=xb[t_i % 2][:, 512 * pc:512 * pc + 512],
                            scalar=nall[:, t_i:t_i + 1], op0=OP.add, in1=fbuf[:, :], op1=OP.add))
                        W("sync", [p3_stt[i3]])
                        p3_yst[i3] = DMA("sync", f"yo{i3 % 4}", lambda s, t_i=t_i, pc=pc: s.dma_start(
                            out=y_out[P * t_i:P * t_i + P, 512 * pc:512 * pc + 512], in_=ya[:, 512 * pc:512 * pc + 512]))
                    if t_i + 2 < TT:
                        W("sync", [p3_stt[t_i * 8 + 7]])
                        t_x1pf[t_i + 2] = DMA("sync", f"x1i{t_i % 2}", lambda s, t_i=t_i: s.dma_start(
                            out=xb[t_i % 2][:, :], in_=x1_scr[P * (t_i + 2):P * (t_i + 2) + P, :]))
        t_yo_all = [(pr.dsems[k][0], pr.dsems[k][1]) for k in pr.dsems if k.startswith("yo")]
        W("sync", t_yo_all)

        for cm in reversed(ps_cms2):
            cm.__exit__(None, None, None)

        block = e(nc.Block())
        pr.build(block)

    return nc


def make_in_map(core, inputs, t_tiles=8):
    C = host_constants()
    TOK = t_tiles * P
    u = np.asarray(inputs["L0_attn_u"]).reshape(N_CORES * 1024, D)
    gate = np.asarray(inputs["L0_ffn_gate"])
    grow = gate[core]
    gateq = np.empty((P, 4), np.float32)
    for k in range(4):
        for p in range(P):
            gateq[p, k] = grow[4 * k + p // 32]
    gater = np.broadcast_to(grow[None, :], (P, NL)).astype(np.float32).copy()
    m = {
        "u": np.ascontiguousarray(u[core * 1024: core * 1024 + TOK]),
        "lam_ri": np.asarray(inputs["L0_attn_Lambda_ri"], np.float32),
        "p_ri": np.asarray(inputs["L0_attn_p_ri"], np.float32),
        "q_ri": np.asarray(inputs["L0_attn_q_ri"], np.float32),
        "vc_ri": np.asarray(inputs["L0_attn_Vc_ri"], np.float32).reshape(NS, NS * 2),
        "ct_ri": np.asarray(inputs["L0_attn_Ct_ri"], np.float32).reshape(NS, 2),
        "bp": np.asarray(inputs["L0_attn_Bp"], np.float32).reshape(1, NS),
        "dsc": np.asarray(inputs["L0_attn_D"], np.float32).reshape(1, 1),
        "lstep": np.asarray(inputs["L0_attn_log_step"], np.float32).reshape(1, 1),
        "wk": np.asarray(inputs["L0_ffn_kernel"], np.float32),
        "bias": np.asarray(inputs["L0_ffn_bias"], np.float32).reshape(1, D),
        "lora_a": np.asarray(inputs["L0_ffn_lora_A"], np.float32).reshape(NL * RANK, D),
        "lora_b": np.asarray(inputs["L0_ffn_lora_B"], np.float32).reshape(NL, D * RANK),
        "lora_as": np.asarray(inputs["L0_ffn_lora_A_shared"], np.float32),
        "lora_bs": np.asarray(inputs["L0_ffn_lora_B_shared"], np.float32).reshape(1, D * RANK),
        "gateq": gateq,
        "gater": gater,
    }
    for name, arr in C.items():
        m[name] = arr
    return m


def kernel(**inputs):
    nc = build_nc(8)
    in_maps = [make_in_map(c, inputs) for c in range(N_CORES)]
    res = run_bass_kernel_spmd(nc, in_maps, core_ids=list(range(N_CORES)))
    out = np.concatenate([np.asarray(res.results[c]["y"]) for c in range(N_CORES)], axis=0)
    return out.reshape(1, N_CORES * 1024, D).astype(np.float32)


# version 20 (final)
# speedup vs baseline: 1.1071x; 1.1071x over previous
from contextlib import ExitStack
import math
import numpy as np
import ml_dtypes
import concourse.bass as bass
import concourse.mybir as mybir
from concourse.bass_utils import run_bass_kernel_spmd

F32 = mybir.dt.float32
BF16 = mybir.dt.bfloat16
AF = mybir.ActivationFunctionType
OP = mybir.AluOpType
AX = mybir.AxisListType.X

P = 128
D = 4096
FC = D // P
NS = 64
RANK = 32
NL = 16
EPS = 1e-5
N_CORES = 8


class Prog:
    ENGS = ["sync", "gpsimd", "vector", "scalar", "tensor"]

    def __init__(self, nc, ctx):
        self.nc = nc
        self.ctx = ctx
        self.ops = {en: [] for en in self.ENGS}
        self.esem = {en: ctx.enter_context(nc.semaphore("tok_" + en))
                     for en in ["vector", "scalar", "tensor"]}
        self.en_cnt = {en: 0 for en in ["vector", "scalar", "tensor"]}
        self.dsems = {}

    def emit(self, en, fn):
        sem = self.esem[en]
        if en in ("vector", "scalar") and self.en_cnt[en] > 0:
            v = self.en_cnt[en]
            self.ops[en].append(lambda e, sem=sem, v=v: e.wait_ge(sem, v))
        self.en_cnt[en] += 1

        def wrapped(e, fn=fn, sem=sem):
            fn(e).then_inc(sem, 1)
        self.ops[en].append(wrapped)
        return (sem, self.en_cnt[en])

    def now(self, en):
        return (self.esem[en], self.en_cnt[en])

    def dma(self, en, stream, fn):
        if stream not in self.dsems:
            self.dsems[stream] = [self.ctx.enter_context(self.nc.semaphore("dma_" + stream)), 0]
        rec = self.dsems[stream]
        sem = rec[0]
        rec[1] += 16

        def wrapped(e, fn=fn, sem=sem):
            fn(e).then_inc(sem, 16)
        self.ops[en].append(wrapped)
        return (sem, rec[1])

    def waits(self, en, toks):
        best = {}
        for t in toks:
            if t is None:
                continue
            sem, v = t
            k = id(sem)
            if k not in best or best[k][1] < v:
                best[k] = (sem, v)
        for sem, v in best.values():
            self.ops[en].append(lambda e, sem=sem, v=v: e.wait_ge(sem, v))

    def build(self, block):
        for en in self.ENGS:
            section = getattr(block, en)
            ops = self.ops[en]

            def make(ops):
                def _(e):
                    for fn in ops:
                        fn(e)
                return _
            section(make(ops))


_CONSTS = None


def host_constants():
    global _CONSTS
    if _CONSTS is not None:
        return _CONSTS
    L = D
    lv = (np.arange(L, dtype=np.float32) / np.float32(L)).astype(np.complex64)
    Omega = np.exp(np.complex64(-2j * np.pi) * lv)
    w = ((np.complex64(1) - Omega) / (np.complex64(1) + Omega)).astype(np.complex64)
    c2 = (np.complex64(2) / (np.complex64(1) + Omega)).astype(np.complex64)

    l1 = np.arange(64)[:, None]
    r1 = np.arange(64)[None, :]
    th = 2.0 * np.pi * l1 * (r1 + 1) / 64.0
    Fre = np.cos(th).astype(np.float32)
    Fim = (-np.sin(th)).astype(np.float32)
    Fimneg = np.sin(th).astype(np.float32)
    r1c = np.arange(64)[:, None]
    l2 = np.arange(64)[None, :]
    thtw = 2.0 * np.pi * l2 * (r1c + 1) / 4096.0
    TWre = np.cos(thtw).astype(np.float32)
    TWim = (-np.sin(thtw)).astype(np.float32)
    l2c = np.arange(64)[:, None]
    r2 = np.arange(64)[None, :]
    thg = 2.0 * np.pi * l2c * r2 / 64.0
    Gre = (np.cos(thg) / 4096.0).astype(np.float32)
    Gimneg = (np.sin(thg) / 4096.0).astype(np.float32)

    _CONSTS = {
        "identb": np.eye(P, dtype=ml_dtypes.bfloat16),
        "identf": np.eye(P, dtype=np.float32),
        "ones2b": np.ones((2, P), dtype=ml_dtypes.bfloat16),
        "ones1f": np.ones((1, P), dtype=np.float32),
        "ones64f": np.ones((1, NS), dtype=np.float32),
        "whre": np.ascontiguousarray(w.real.reshape(1, L)).astype(np.float32),
        "whim": np.ascontiguousarray(w.imag.reshape(1, L)).astype(np.float32),
        "c2re": np.ascontiguousarray(c2.real.reshape(P, 32)).astype(np.float32),
        "c2im": np.ascontiguousarray(c2.imag.reshape(P, 32)).astype(np.float32),
        "Fre": Fre, "Fim": Fim, "Fimneg": Fimneg,
        "TWre": TWre, "TWim": TWim,
        "Gre": Gre, "Gimneg": Gimneg,
    }
    return _CONSTS


def build_nc(t_tiles=8):
    nc = bass.Bass("TRN2", target_bir_lowering=False)
    TT = t_tiles
    TOK = TT * P

    u = nc.dram_tensor("u", [TOK, D], F32, kind="ExternalInput").ap()
    lam_ri = nc.dram_tensor("lam_ri", [NS, 2], F32, kind="ExternalInput").ap()
    p_ri = nc.dram_tensor("p_ri", [NS, 2], F32, kind="ExternalInput").ap()
    q_ri = nc.dram_tensor("q_ri", [NS, 2], F32, kind="ExternalInput").ap()
    vc_ri = nc.dram_tensor("vc_ri", [NS, NS * 2], F32, kind="ExternalInput").ap()
    ct_ri = nc.dram_tensor("ct_ri", [NS, 2], F32, kind="ExternalInput").ap()
    bp_in = nc.dram_tensor("bp", [1, NS], F32, kind="ExternalInput").ap()
    dsc = nc.dram_tensor("dsc", [1, 1], F32, kind="ExternalInput").ap()
    lstep = nc.dram_tensor("lstep", [1, 1], F32, kind="ExternalInput").ap()
    wk = nc.dram_tensor("wk", [D, D], F32, kind="ExternalInput").ap()
    bias_in = nc.dram_tensor("bias", [1, D], F32, kind="ExternalInput").ap()
    lora_a = nc.dram_tensor("lora_a", [NL * RANK, D], F32, kind="ExternalInput").ap()
    lora_b = nc.dram_tensor("lora_b", [NL, D * RANK], F32, kind="ExternalInput").ap()
    lora_as = nc.dram_tensor("lora_as", [RANK, D], F32, kind="ExternalInput").ap()
    lora_bs = nc.dram_tensor("lora_bs", [1, D * RANK], F32, kind="ExternalInput").ap()
    gateq = nc.dram_tensor("gateq", [P, 4], F32, kind="ExternalInput").ap()
    gater = nc.dram_tensor("gater", [P, NL], F32, kind="ExternalInput").ap()

    C = host_constants()
    cin = {}
    for name, arr in C.items():
        dt = BF16 if arr.dtype == ml_dtypes.bfloat16 else F32
        cin[name] = nc.dram_tensor(name, list(arr.shape), dt, kind="ExternalInput").ap()

    y_out = nc.dram_tensor("y", [TOK, D], F32, kind="ExternalOutput").ap()

    x1_scr = nc.dram_tensor("x1_scr", [TOK, D], F32).ap()
    f_scr = nc.dram_tensor("f_scr", [TOK, D], F32).ap()
    krev_t = nc.dram_tensor("krev_scr", [1, 5120], BF16)
    ar_scr = nc.dram_tensor("ar_scr", [2, D], F32).ap()
    k_scr = nc.dram_tensor("k_scr", [16, D], F32).ap()
    bias_scr = nc.dram_tensor("bias_scr", [2, D], BF16).ap()
    bpad_scr_t = nc.dram_tensor("bpad_scr", [D, P], BF16)
    bpad_scr = bpad_scr_t.ap()

    with ExitStack() as ctx:
        e = ctx.enter_context

        xb = [e(nc.sbuf_tensor("xb0", [P, D], F32)), e(nc.sbuf_tensor("xb1", [P, D], F32))]
        ya = e(nc.sbuf_tensor("ya", [P, D], F32))
        n1b = e(nc.sbuf_tensor("n1b", [P, D], BF16))
        xfmt = e(nc.sbuf_tensor("xfmt", [P, D], BF16))
        trv = e(nc.sbuf_tensor("trv", [P, D], BF16))
        tra = e(nc.sbuf_tensor("tra", [P, D], BF16))
        tband = e(nc.sbuf_tensor("tband", [P, 4480], BF16))
        n2fm = e(nc.sbuf_tensor("n2fm", [P, FC * TOK], BF16))
        if FC * TOK >= 4480:
            tbandr = n2fm
        else:
            tbandr = e(nc.sbuf_tensor("tbandr", [P, 4480], BF16))
        astkT = e(nc.sbuf_tensor("astkT", [P, FC * NS], BF16))
        bstkT = e(nc.sbuf_tensor("bstkT", [2 * RANK, D], BF16))
        axb = e(nc.sbuf_tensor("axb", [2 * RANK, TOK], BF16))
        bhl = e(nc.sbuf_tensor("bhl", [2, D], BF16))
        wbuf = [e(nc.sbuf_tensor(f"wbuf{i}", [P, 512], BF16)) for i in range(4)]
        fstage = [e(nc.sbuf_tensor(f"fstage{i}", [P, 512], F32)) for i in range(4)]

        idsb = e(nc.sbuf_tensor("idsb", [P, P], BF16))
        idsf = e(nc.sbuf_tensor("idsf", [P, P], F32))
        ones2bs = e(nc.sbuf_tensor("ones2bs", [2, P], BF16))
        ones1s = e(nc.sbuf_tensor("ones1s", [1, P], F32))
        ones64s = e(nc.sbuf_tensor("ones64s", [1, NS], F32))
        lam_s = e(nc.sbuf_tensor("lam_s", [NS, 2], F32))
        pq_s = e(nc.sbuf_tensor("pq_s", [NS, 4], F32))
        ct_s = e(nc.sbuf_tensor("ct_s", [NS, 2], F32))
        vc_s = e(nc.sbuf_tensor("vc_s", [NS, NS * 2], F32))
        bp_s = e(nc.sbuf_tensor("bp_s", [1, NS], F32))
        bpb = e(nc.sbuf_tensor("bpb", [NS, NS], F32))
        bvec = e(nc.sbuf_tensor("bvec", [NS, 2], F32))
        vmat = e(nc.sbuf_tensor("vmat", [NS, 8], F32))
        tmp641 = e(nc.sbuf_tensor("tmp641", [NS, NS], F32))
        sstep = e(nc.sbuf_tensor("sstep", [1, 1], F32))
        lnbias = e(nc.sbuf_tensor("lnbias", [NS, 1], F32))
        stepb = e(nc.sbuf_tensor("stepb", [NS, 1], F32))
        dbc = e(nc.sbuf_tensor("dbc", [P, 1], F32))
        dtile = e(nc.sbuf_tensor("dtile", [1, 1], F32))
        gq_s = e(nc.sbuf_tensor("gq_s", [P, 4], F32))
        gr_s = e(nc.sbuf_tensor("gr_s", [P, NL], F32))

        kt = [e(nc.sbuf_tensor(f"kt{i}", [P, 32], F32)) for i in range(16)]
        cc = [e(nc.sbuf_tensor(f"cc{i}", [P, 32], F32)) for i in range(6)]
        arre = e(nc.sbuf_tensor("arre", [P, 32], F32))
        arim = e(nc.sbuf_tensor("arim", [P, 32], F32))
        c2re_s = e(nc.sbuf_tensor("c2re_s", [P, 32], F32))
        c2im_s = e(nc.sbuf_tensor("c2im_s", [P, 32], F32))

        fft_s = {nm: e(nc.sbuf_tensor(nm + "_s", [NS, NS], F32))
                 for nm in ["Fre", "Fim", "Fimneg", "TWre", "TWim", "Gre", "Gimneg"]}
        arm_re = e(nc.sbuf_tensor("arm_re", [NS, NS], F32))
        arm_im = e(nc.sbuf_tensor("arm_im", [NS, NS], F32))
        Are = e(nc.sbuf_tensor("Are", [NS, NS], F32))
        Aim = e(nc.sbuf_tensor("Aim", [NS, NS], F32))
        Bre = e(nc.sbuf_tensor("BreT", [NS, NS], F32))
        Bim = e(nc.sbuf_tensor("BimT", [NS, NS], F32))
        Btre = e(nc.sbuf_tensor("Btre", [NS, NS], F32))
        Btim = e(nc.sbuf_tensor("Btim", [NS, NS], F32))
        kmatb = e(nc.sbuf_tensor("kmatb", [NS, NS], BF16))
        kmatbT = e(nc.sbuf_tensor("kmatbT", [NS, NS], BF16))
        wh_s = [e(nc.sbuf_tensor(f"wh_s{i}", [1, 512], F32)) for i in range(2)]
        b128 = e(nc.sbuf_tensor("b128", [P, 32], F32))
        bhi = e(nc.sbuf_tensor("bhi", [P, 32], BF16))
        bhif = e(nc.sbuf_tensor("bhif", [P, 32], F32))
        blo = e(nc.sbuf_tensor("blo", [P, 32], BF16))

        def stat_site(tag):
            return {nm: e(nc.sbuf_tensor(f"{tag}_{nm}", [P, 1], F32))
                    for nm in ["s1", "q1", "mean", "var", "rstd", "nbias"]}
        st1 = stat_site("st1")
        sta = stat_site("sta")
        st2 = stat_site("st2")
        stf = stat_site("stf")
        nall = e(nc.sbuf_tensor("nall", [P, 8], F32))
        yacols = e(nc.sbuf_tensor("yacols", [P, 8], F32))
        yaqcols = e(nc.sbuf_tensor("yaqcols", [P, 8], F32))
        fsums = [e(nc.sbuf_tensor(f"fsums{t}", [P, 8], F32)) for t in range(TT)]
        fsqs = [e(nc.sbuf_tensor(f"fsqs{t}", [P, 8], F32)) for t in range(TT)]

        pr = Prog(nc, ctx)
        E = pr.emit
        W = pr.waits
        DMA = pr.dma

        ps_cms = []

        def psum_open(name, shape, dt):
            cm = nc.psum_tensor(name, shape, dt)
            ps_cms.append(cm)
            return cm.__enter__()

        putil = psum_open("putil", [P, 512], F32)
        pconv = [psum_open(f"pconv{i}", [P, 512], F32) for i in range(3)]
        ptrb = [psum_open(f"ptrb{i}", [P, 1024], BF16) for i in range(2)]
        pax = psum_open("pax", [2 * RANK, TOK], F32)

        tc = {}
        const_loads = [
            ("identb", idsb), ("identf", idsf), ("ones2b", ones2bs),
            ("ones1f", ones1s), ("ones64f", ones64s),
            ("c2re", c2re_s), ("c2im", c2im_s),
        ]
        for name, dst in const_loads:
            tc[name] = DMA("sync", "consts", lambda s, d=dst, src=cin[name]: s.dma_start(out=d[:, :], in_=src))
        for nm in fft_s:
            tc[nm] = DMA("sync", "consts", lambda s, d=fft_s[nm], src=cin[nm]: s.dma_start(out=d[:, :], in_=src))
        tc["gateq"] = DMA("sync", "consts", lambda s: s.dma_start(out=gq_s[:, :], in_=gateq))
        tc["gater"] = DMA("sync", "consts", lambda s: s.dma_start(out=gr_s[:, :], in_=gater))

        tsm = []
        for src, dst in [(lam_ri, lam_s[:, :]), (ct_ri, ct_s[:, :]), (vc_ri, vc_s[:, :]),
                         (bp_in, bp_s[:, :]), (dsc, dtile[:, :]), (lstep, sstep[:, :])]:
            tsm.append(DMA("sync", "smalls", lambda s, src=src, dst=dst: s.dma_start(out=dst, in_=src)))
        tsm.append(DMA("sync", "smalls", lambda s: s.dma_start(out=pq_s[:, 0:2], in_=p_ri)))
        tsm.append(DMA("sync", "smalls", lambda s: s.dma_start(out=pq_s[:, 2:4], in_=q_ri)))
        tsm.append(DMA("sync", "smalls", lambda s: s.dma_start(
            out=b128[:, :], in_=bias_in.rearrange("1 (p c) -> p c", p=P))))
        t_call = (pr.dsems["consts"][0], pr.dsems["consts"][1])
        t_sall = (pr.dsems["smalls"][0], pr.dsems["smalls"][1])
        for _k in list(tc):
            tc[_k] = t_call
        tsm = [t_sall]

        W("scalar", tsm)
        t_exp = E("scalar", lambda s: s.activation(out=sstep[:, :], in_=sstep[:, :], func=AF.Exp, scale=-1.0))
        W("vector", [t_exp])
        t_2os = E("vector", lambda v: v.tensor_scalar(out=sstep[:, :], in0=sstep[:, :], scalar1=2.0, scalar2=None, op0=OP.mult))
        W("tensor", [tc["ones64f"], t_2os])
        t_mm1 = E("tensor", lambda t: t.matmul(out=putil[0:NS, 0:1], lhsT=ones64s[:, :], rhs=sstep[:, :], start=True, stop=True))
        W("vector", [t_mm1])
        t_stepb = E("vector", lambda v: v.tensor_copy(out=stepb[:, :], in_=putil[0:NS, 0:1]))

        W("tensor", [tc["ones1f"], t_stepb] + tsm)
        t_mm2 = E("tensor", lambda t: t.matmul(out=putil[:, 1:2], lhsT=ones1s[:, :], rhs=dtile[:, :], start=True, stop=True))
        W("vector", [t_mm2])
        t_dbc = E("vector", lambda v: v.tensor_copy(out=dbc[:, :], in_=putil[:, 1:2]))

        W("tensor", [t_dbc])
        t_mm3 = E("tensor", lambda t: t.matmul(out=putil[0:NS, 0:NS], lhsT=ones64s[:, :], rhs=bp_s[:, :], start=True, stop=True))
        W("vector", [t_mm3])
        t_bpb = E("vector", lambda v: v.tensor_copy(out=bpb[:, :], in_=putil[0:NS, 0:NS]))

        W("vector", [t_bpb])
        t1 = E("vector", lambda v: v.tensor_tensor(out=tmp641[:, :], in0=bass.AP(vc_s, 0, [[NS * 2, NS], [2, NS]]), in1=bpb[:, :], op=OP.mult))
        W("vector", [t1])
        t_bre = E("vector", lambda v: v.reduce_sum(out=bvec[:, 0:1], in_=tmp641[:, :], axis=AX))
        t2 = E("vector", lambda v: v.tensor_tensor(out=tmp641[:, :], in0=bass.AP(vc_s, 1, [[NS * 2, NS], [2, NS]]), in1=bpb[:, :], op=OP.mult))
        W("vector", [t2])
        t_bim = E("vector", lambda v: v.reduce_sum(out=bvec[:, 1:2], in_=tmp641[:, :], axis=AX))
        W("vector", [t_bre, t_bim])

        def cmul_conj(col, are, aim, bre, bim):
            ta = E("vector", lambda v: v.tensor_tensor(out=vmat[:, 2 * col:2 * col + 1], in0=are, in1=bre, op=OP.mult))
            tb = E("vector", lambda v: v.tensor_tensor(out=tmp641[:, 0:1], in0=aim, in1=bim, op=OP.mult))
            W("vector", [ta, tb])
            E("vector", lambda v: v.tensor_tensor(out=vmat[:, 2 * col:2 * col + 1], in0=vmat[:, 2 * col:2 * col + 1], in1=tmp641[:, 0:1], op=OP.add))
            tcsh = E("vector", lambda v: v.tensor_tensor(out=vmat[:, 2 * col + 1:2 * col + 2], in0=are, in1=bim, op=OP.mult))
            td = E("vector", lambda v: v.tensor_tensor(out=tmp641[:, 1:2], in0=aim, in1=bre, op=OP.mult))
            W("vector", [tcsh, td])
            return E("vector", lambda v: v.tensor_tensor(out=vmat[:, 2 * col + 1:2 * col + 2], in0=vmat[:, 2 * col + 1:2 * col + 2], in1=tmp641[:, 1:2], op=OP.subtract))

        t_v0 = cmul_conj(0, ct_s[:, 0:1], ct_s[:, 1:2], bvec[:, 0:1], bvec[:, 1:2])
        t_v1 = cmul_conj(1, ct_s[:, 0:1], ct_s[:, 1:2], pq_s[:, 0:1], pq_s[:, 1:2])
        t_v2 = cmul_conj(2, pq_s[:, 2:3], pq_s[:, 3:4], bvec[:, 0:1], bvec[:, 1:2])
        t_v3 = cmul_conj(3, pq_s[:, 2:3], pq_s[:, 3:4], pq_s[:, 0:1], pq_s[:, 1:2])
        t_vv = [t_v0, t_v1, t_v2, t_v3]

        gb_toks = []
        gb_mms = []
        prev_sc = None
        pot = [putil, pconv[0], pconv[1]]
        kwh = 0
        for half, dst in [(0, xb[0]), (1, xb[1])]:
            for cchunk in range(8):
                src_c = cin["whre"] if half == 0 else cin["whim"]
                if kwh >= 2:
                    W("sync", [gb_mms[kwh - 2]])
                t_wh = DMA("sync", f"wh{kwh % 2}", lambda s, src_c=src_c, cchunk=cchunk, kwh=kwh: s.dma_start(
                    out=wh_s[kwh % 2][:, :], in_=src_c[0:1, 512 * cchunk:512 * cchunk + 512]))
                toks = [t_wh, t_stepb, t_bpb]
                if kwh >= 3:
                    toks.append(gb_toks[kwh - 3])
                W("tensor", toks)
                tmm = E("tensor", lambda t, kwh=kwh: t.matmul(
                    out=pot[kwh % 3][0:NS, :], lhsT=ones64s[:, :],
                    rhs=wh_s[kwh % 2][:, :], start=True, stop=True))
                gb_mms.append(tmm)
                W("scalar", [tmm])
                prev_sc = E("scalar", lambda s, dst=dst, cchunk=cchunk, kwh=kwh: s.activation(
                    out=dst[0:NS, 512 * cchunk:512 * cchunk + 512], in_=pot[kwh % 3][0:NS, :],
                    func=AF.Identity, scale=stepb[:, :]))
                gb_toks.append(prev_sc)
                kwh += 1

        W("vector", gb_toks + t_vv)
        t_dre = E("vector", lambda v: v.tensor_scalar(out=xb[0][0:NS, :], in0=xb[0][0:NS, :], scalar1=lam_s[:, 0:1], scalar2=None, op0=OP.subtract))
        t_dim = E("vector", lambda v: v.tensor_scalar(out=xb[1][0:NS, :], in0=xb[1][0:NS, :], scalar1=lam_s[:, 1:2], scalar2=None, op0=OP.subtract))
        W("scalar", [t_dre])
        t_s1 = E("scalar", lambda s: s.activation(out=ya[0:NS, :], in_=xb[0][0:NS, :], func=AF.Square))
        W("scalar", [t_dim])
        t_s2 = E("scalar", lambda s: s.activation(out=xb[1][0:NS, :], in_=xb[1][0:NS, :], func=AF.Square))
        W("vector", [t_s1, t_s2])
        t_s12 = E("vector", lambda v: v.tensor_tensor(out=ya[0:NS, :], in0=ya[0:NS, :], in1=xb[1][0:NS, :], op=OP.add))
        W("scalar", [t_s12])
        t_ln = E("scalar", lambda s: s.activation(out=ya[0:NS, :], in_=ya[0:NS, :], func=AF.Ln, scale=float(2.0 ** -32)))
        W("scalar", [t_ln])
        t_lb = E("vector", lambda v: v.memset(lnbias[:, :], float(-32.0 * math.log(2.0))))
        W("scalar", [t_lb])
        t_rec = E("scalar", lambda s: s.activation(out=ya[0:NS, :], in_=ya[0:NS, :], func=AF.Exp, scale=-1.0, bias=lnbias[:, :]))
        W("vector", [t_rec])
        t_rdre = E("vector", lambda v: v.tensor_tensor(out=xb[0][0:NS, :], in0=xb[0][0:NS, :], in1=ya[0:NS, :], op=OP.mult))
        prev_sc2 = t_s12
        for cchunk in range(8):
            if kwh >= 2:
                W("sync", [gb_mms[kwh - 2]])
            t_wh = DMA("sync", f"wh{kwh % 2}", lambda s, cchunk=cchunk, kwh=kwh: s.dma_start(
                out=wh_s[kwh % 2][:, :], in_=cin["whim"][0:1, 512 * cchunk:512 * cchunk + 512]))
            W("tensor", [t_wh, gb_toks[kwh - 3], t_s12 if cchunk == 0 else prev_sc2])
            tmm = E("tensor", lambda t, kwh=kwh: t.matmul(
                out=pot[kwh % 3][0:NS, :], lhsT=ones64s[:, :],
                rhs=wh_s[kwh % 2][:, :], start=True, stop=True))
            gb_mms.append(tmm)
            W("scalar", [tmm])
            prev_sc2 = E("scalar", lambda s, cchunk=cchunk, kwh=kwh: s.activation(
                out=xb[1][0:NS, 512 * cchunk:512 * cchunk + 512], in_=pot[kwh % 3][0:NS, :],
                func=AF.Identity, scale=stepb[:, :]))
            gb_toks.append(prev_sc2)
            kwh += 1
        W("vector", [prev_sc2])
        t_dim2 = E("vector", lambda v: v.tensor_scalar(out=xb[1][0:NS, :], in0=xb[1][0:NS, :], scalar1=lam_s[:, 1:2], scalar2=None, op0=OP.subtract))
        t_rdim = E("vector", lambda v: v.scalar_tensor_tensor(out=xb[1][0:NS, :], in0=xb[1][0:NS, :], scalar=-1.0, op0=OP.mult, in1=ya[0:NS, :], op1=OP.mult))

        prev_ev = None
        ks_evs = []
        kk = 0
        for src, outrow, tok in [(xb[0], 0, t_rdre), (xb[1], 32, t_rdim)]:
            for cchunk in range(8):
                toks = [tok, t_rec, t_rdim]
                if kk >= 3:
                    toks.append(ks_evs[kk - 3])
                else:
                    toks.append(gb_toks[-1])
                W("tensor", toks)
                tmm = E("tensor", lambda t, src=src, cchunk=cchunk, kk=kk: t.matmul(
                    out=pot[kk % 3][0:8, :], lhsT=vmat[:, :], rhs=src[0:NS, 512 * cchunk:512 * cchunk + 512],
                    start=True, stop=True))
                W("vector", [tmm])
                prev_ev = E("vector", lambda v, outrow=outrow, cchunk=cchunk, kk=kk: v.tensor_copy(
                    out=ya[outrow:outrow + 8, 512 * cchunk:512 * cchunk + 512], in_=pot[kk % 3][0:8, :]))
                ks_evs.append(prev_ev)
                kk += 1
        t_ksum_done = prev_ev
        W("gpsimd", [t_ksum_done])
        t_ks1 = DMA("gpsimd", "kscr", lambda g: g.dma_start(out=k_scr[0:8, :], in_=ya[0:8, :]))
        t_ks2 = DMA("gpsimd", "kscr", lambda g: g.dma_start(out=k_scr[8:16, :], in_=ya[32:40, :]))
        W("gpsimd", [t_ks1, t_ks2])
        t_ktl = []
        for i in range(16):
            t_ktl.append(DMA("gpsimd", "kscr", lambda g, i=i: g.dma_start(
                out=kt[i][:, :], in_=k_scr[i:i + 1, :].rearrange("1 (p c) -> p c", p=P))))

        W("vector", t_ktl + [tc["c2re"], tc["c2im"]])
        TTE = lambda fn: E("vector", fn)
        for k in range(4):
            a = TTE(lambda v, k=k: v.tensor_tensor(out=kt[2 * k][:, :], in0=kt[2 * k][:, :], in1=kt[8 + 2 * k + 1][:, :], op=OP.subtract))
            b = TTE(lambda v, k=k: v.tensor_tensor(out=kt[8 + 2 * k][:, :], in0=kt[8 + 2 * k][:, :], in1=kt[2 * k + 1][:, :], op=OP.add))
            W("vector", [a, b])
        t = TTE(lambda v: v.tensor_scalar(out=kt[6][:, :], in0=kt[6][:, :], scalar1=1.0, scalar2=None, op0=OP.add))
        W("vector", [t])
        t = TTE(lambda v: v.tensor_tensor(out=cc[0][:, :], in0=kt[6][:, :], in1=kt[6][:, :], op=OP.mult))
        t2b = TTE(lambda v: v.tensor_tensor(out=cc[1][:, :], in0=kt[14][:, :], in1=kt[14][:, :], op=OP.mult))
        W("vector", [t, t2b])
        t = TTE(lambda v: v.tensor_tensor(out=cc[0][:, :], in0=cc[0][:, :], in1=cc[1][:, :], op=OP.add))
        W("vector", [t])
        t_recd = TTE(lambda v: v.reciprocal(out=cc[0][:, :], in_=cc[0][:, :]))
        a1 = TTE(lambda v: v.tensor_tensor(out=cc[1][:, :], in0=kt[2][:, :], in1=kt[4][:, :], op=OP.mult))
        a2 = TTE(lambda v: v.tensor_tensor(out=cc[2][:, :], in0=kt[10][:, :], in1=kt[12][:, :], op=OP.mult))
        W("vector", [a1, a2])
        t_numre = TTE(lambda v: v.tensor_tensor(out=cc[1][:, :], in0=cc[1][:, :], in1=cc[2][:, :], op=OP.subtract))
        a3 = TTE(lambda v: v.tensor_tensor(out=cc[2][:, :], in0=kt[2][:, :], in1=kt[12][:, :], op=OP.mult))
        a4 = TTE(lambda v: v.tensor_tensor(out=cc[3][:, :], in0=kt[10][:, :], in1=kt[4][:, :], op=OP.mult))
        W("vector", [a3, a4])
        t_numim = TTE(lambda v: v.tensor_tensor(out=cc[2][:, :], in0=cc[2][:, :], in1=cc[3][:, :], op=OP.add))
        W("vector", [t_numre, t_numim, t_recd])
        b1 = TTE(lambda v: v.tensor_tensor(out=cc[3][:, :], in0=cc[1][:, :], in1=kt[6][:, :], op=OP.mult))
        b2 = TTE(lambda v: v.tensor_tensor(out=cc[4][:, :], in0=cc[2][:, :], in1=kt[14][:, :], op=OP.mult))
        W("vector", [b1, b2])
        t = TTE(lambda v: v.tensor_tensor(out=cc[3][:, :], in0=cc[3][:, :], in1=cc[4][:, :], op=OP.add))
        W("vector", [t])
        t_rre = TTE(lambda v: v.tensor_tensor(out=cc[3][:, :], in0=cc[3][:, :], in1=cc[0][:, :], op=OP.mult))
        b3 = TTE(lambda v: v.tensor_tensor(out=cc[4][:, :], in0=cc[2][:, :], in1=kt[6][:, :], op=OP.mult))
        b4 = TTE(lambda v: v.tensor_tensor(out=cc[5][:, :], in0=cc[1][:, :], in1=kt[14][:, :], op=OP.mult))
        W("vector", [b3, b4])
        t = TTE(lambda v: v.tensor_tensor(out=cc[4][:, :], in0=cc[4][:, :], in1=cc[5][:, :], op=OP.subtract))
        W("vector", [t])
        t_rim = TTE(lambda v: v.tensor_tensor(out=cc[4][:, :], in0=cc[4][:, :], in1=cc[0][:, :], op=OP.mult))
        W("vector", [t_rre, t_rim])
        t_dfre = TTE(lambda v: v.tensor_tensor(out=cc[1][:, :], in0=kt[0][:, :], in1=cc[3][:, :], op=OP.subtract))
        t_dfim = TTE(lambda v: v.tensor_tensor(out=cc[2][:, :], in0=kt[8][:, :], in1=cc[4][:, :], op=OP.subtract))
        W("vector", [t_dfre, t_dfim])
        m1 = TTE(lambda v: v.tensor_tensor(out=arre[:, :], in0=c2re_s[:, :], in1=cc[1][:, :], op=OP.mult))
        m2 = TTE(lambda v: v.tensor_tensor(out=cc[5][:, :], in0=c2im_s[:, :], in1=cc[2][:, :], op=OP.mult))
        W("vector", [m1, m2])
        t_arre = TTE(lambda v: v.tensor_tensor(out=arre[:, :], in0=arre[:, :], in1=cc[5][:, :], op=OP.subtract))
        m3 = TTE(lambda v: v.tensor_tensor(out=arim[:, :], in0=c2re_s[:, :], in1=cc[2][:, :], op=OP.mult))
        m4 = TTE(lambda v: v.tensor_tensor(out=cc[5][:, :], in0=c2im_s[:, :], in1=cc[1][:, :], op=OP.mult))
        W("vector", [m3, m4])
        t_arim = TTE(lambda v: v.tensor_tensor(out=arim[:, :], in0=arim[:, :], in1=cc[5][:, :], op=OP.add))

        W("gpsimd", [t_arre, t_arim])
        t_ar1 = DMA("gpsimd", "kscr", lambda g: g.dma_start(
            out=ar_scr[0:1, :].rearrange("1 (p c) -> p c", p=P), in_=arre[:, :]))
        t_ar2 = DMA("gpsimd", "kscr", lambda g: g.dma_start(
            out=ar_scr[1:2, :].rearrange("1 (p c) -> p c", p=P), in_=arim[:, :]))
        W("gpsimd", [t_ar1, t_ar2])
        t_arm1 = DMA("gpsimd", "kscr", lambda g: g.dma_start(
            out=arm_re[:, :], in_=ar_scr[0:1, :].rearrange("1 (p c) -> p c", p=NS)))
        t_arm2 = DMA("gpsimd", "kscr", lambda g: g.dma_start(
            out=arm_im[:, :], in_=ar_scr[1:2, :].rearrange("1 (p c) -> p c", p=NS)))

        W("tensor", [t_arm1, t_arm2, t_ksum_done] + [tc[nm] for nm in fft_s])
        tm1 = E("tensor", lambda t: t.matmul(out=putil[0:NS, 0:NS], lhsT=fft_s["Fre"][:, :], rhs=arm_re[:, :], start=True, stop=False))
        tm2 = E("tensor", lambda t: t.matmul(out=putil[0:NS, 0:NS], lhsT=fft_s["Fimneg"][:, :], rhs=arm_im[:, :], start=False, stop=True))
        W("vector", [tm2])
        t_are = E("vector", lambda v: v.tensor_copy(out=Are[:, :], in_=putil[0:NS, 0:NS]))
        W("tensor", [t_are])
        tm3 = E("tensor", lambda t: t.matmul(out=putil[0:NS, 0:NS], lhsT=fft_s["Fim"][:, :], rhs=arm_re[:, :], start=True, stop=False))
        tm4 = E("tensor", lambda t: t.matmul(out=putil[0:NS, 0:NS], lhsT=fft_s["Fre"][:, :], rhs=arm_im[:, :], start=False, stop=True))
        W("vector", [tm4])
        t_aim = E("vector", lambda v: v.tensor_copy(out=Aim[:, :], in_=putil[0:NS, 0:NS]))
        W("vector", [t_are, t_aim])
        w1 = E("vector", lambda v: v.tensor_tensor(out=Bre[:, :], in0=Are[:, :], in1=fft_s["TWre"][:, :], op=OP.mult))
        w2 = E("vector", lambda v: v.tensor_tensor(out=tmp641[:, :], in0=Aim[:, :], in1=fft_s["TWim"][:, :], op=OP.mult))
        W("vector", [w1, w2])
        t_bre2 = E("vector", lambda v: v.tensor_tensor(out=Bre[:, :], in0=Bre[:, :], in1=tmp641[:, :], op=OP.subtract))
        w3 = E("vector", lambda v: v.tensor_tensor(out=Bim[:, :], in0=Are[:, :], in1=fft_s["TWim"][:, :], op=OP.mult))
        w4 = E("vector", lambda v: v.tensor_tensor(out=tmp641[:, :], in0=Aim[:, :], in1=fft_s["TWre"][:, :], op=OP.mult))
        W("vector", [w3, w4])
        t_bim2 = E("vector", lambda v: v.tensor_tensor(out=Bim[:, :], in0=Bim[:, :], in1=tmp641[:, :], op=OP.add))
        W("tensor", [t_bre2, tc["identf"]])
        tt1 = E("tensor", lambda t: t.transpose(out=putil[0:NS, 0:NS], in_=Bre[:, :], identity=idsf[0:NS, 0:NS]))
        W("vector", [tt1])
        t_btre = E("vector", lambda v: v.tensor_copy(out=Btre[:, :], in_=putil[0:NS, 0:NS]))
        W("tensor", [t_bim2, t_btre])
        tt2 = E("tensor", lambda t: t.transpose(out=putil[0:NS, 0:NS], in_=Bim[:, :], identity=idsf[0:NS, 0:NS]))
        W("vector", [tt2])
        t_btim = E("vector", lambda v: v.tensor_copy(out=Btim[:, :], in_=putil[0:NS, 0:NS]))
        W("tensor", [t_btim])
        tm5 = E("tensor", lambda t: t.matmul(out=putil[0:NS, 0:NS], lhsT=Btre[:, :], rhs=fft_s["Gre"][:, :], start=True, stop=False))
        tm6 = E("tensor", lambda t: t.matmul(out=putil[0:NS, 0:NS], lhsT=Btim[:, :], rhs=fft_s["Gimneg"][:, :], start=False, stop=True))
        W("vector", [tm6])
        t_kmb = E("vector", lambda v: v.tensor_copy(out=kmatb[:, :], in_=putil[0:NS, 0:NS]))
        W("tensor", [t_kmb, tc["identb"]])
        tt3 = E("tensor", lambda t: t.transpose(out=ptrb[0][0:NS, 0:NS], in_=kmatb[:, :], identity=idsb[0:NS, 0:NS]))
        W("vector", [tt3])
        t_kmbT = E("vector", lambda v: v.tensor_copy(out=kmatbT[:, :], in_=ptrb[0][0:NS, 0:NS]))
        ztile = tband
        t_zt = E("vector", lambda v: v.memset(ztile[0:1, 0:512], 0.0))
        W("gpsimd", [t_kmbT, t_zt])
        t_kv1 = DMA("gpsimd", "kscr", lambda g: g.dma_start(out=bass.AP(krev_t, 512, [[NS, NS], [1, NS]]), in_=kmatbT[:, :]))
        t_kv2 = DMA("gpsimd", "kscr", lambda g: g.dma_start(out=bass.AP(krev_t, 0, [[512, 1], [1, 512]]), in_=ztile[0:1, 0:512]))
        t_kv3 = DMA("gpsimd", "kscr", lambda g: g.dma_start(out=bass.AP(krev_t, 4608, [[512, 1], [1, 512]]), in_=ztile[0:1, 0:512]))
        W("gpsimd", [t_kv1, t_kv2, t_kv3])
        _rpitch = FC * TOK if FC * TOK >= 4480 else 4480
        t_tbr = DMA("gpsimd", "kscr", lambda g: g.dma_start(
            out=bass.AP(tbandr, 0, [[_rpitch, P], [1, 4480]]), in_=bass.AP(krev_t, 512, [[1, P], [1, 4480]])))
        W("vector", [t_tbr])
        t_tband = E("vector", lambda v: v.tensor_copy(
            out=tband[:, :], in_=bass.AP(tbandr, 4479, [[_rpitch, P], [-1, 4480]])))

        abuf = [xfmt, trv]
        t_acc = None
        t_lda_prev = []
        t_accs = []
        for k in range(4):
            toks = [t_ks1, t_ks2]
            if k >= 2:
                W("gpsimd", [t_accs[k - 2]])
            t_ld = DMA("gpsimd", f"lora{k % 2}", lambda g, k=k, b=abuf[k % 2]: g.dma_start(
                out=b[:, :], in_=lora_a[128 * k:128 * k + 128, :]))
            W("vector", [t_ld, tc["gateq"]] + toks + ([t_acc] if t_acc else []))
            if k == 0:
                t_acc = E("vector", lambda v, b=abuf[0]: v.tensor_scalar(
                    out=ya[:, :], in0=b[:, :], scalar1=gq_s[:, 0:1], scalar2=None, op0=OP.mult))
            else:
                t_acc = E("vector", lambda v, k=k, b=abuf[k % 2]: v.scalar_tensor_tensor(
                    out=ya[:, :], in0=b[:, :], scalar=gq_s[:, k:k + 1], op0=OP.mult, in1=ya[:, :], op1=OP.add))
            t_accs.append(t_acc)
            t_lda_prev.append(t_ld)
        W("vector", [t_acc])
        t_cp1 = E("vector", lambda v: v.tensor_copy(out=xb[1][0:NS, :], in_=ya[NS:P, :]))
        t_cb1 = E("vector", lambda v: v.tensor_tensor(out=ya[0:NS, :], in0=ya[0:NS, :], in1=xb[1][0:NS, :], op=OP.add))
        t_cp2 = E("vector", lambda v: v.tensor_copy(out=xb[1][0:RANK, :], in_=ya[RANK:NS, :]))
        t_cb2 = E("vector", lambda v: v.tensor_tensor(out=ya[0:RANK, :], in0=ya[0:RANK, :], in1=xb[1][0:RANK, :], op=OP.add))
        W("scalar", [t_cb2])
        t_acast = E("scalar", lambda s: s.activation(out=trv[0:RANK, :], in_=ya[0:RANK, :], func=AF.Identity))
        W("gpsimd", t_lda_prev + [t_acc])
        t_ash = DMA("gpsimd", "lash", lambda g: g.dma_start(out=trv[RANK:NS, :], in_=lora_as))
        t_aev_prev = None
        t_atr_last = None
        for r in range(4):
            toks = [t_acast, t_ash, tc["identb"], t_tband, t_kmbT]
            if t_aev_prev is not None:
                toks.append(t_aev_prev)
            W("tensor", toks)
            for j in range(8):
                c = 8 * r + j
                t_atr_last = E("tensor", lambda t, c=c, r=r, j=j: t.transpose(
                    out=ptrb[r % 2][:, 64 * j:64 * j + 64], in_=trv[0:NS, 128 * c:128 * c + 128],
                    identity=idsb[0:NS, 0:NS]))
            W("vector", [t_atr_last])
            t_aev_prev = E("vector", lambda v, r=r: v.tensor_copy(
                out=astkT[:, 512 * r:512 * r + 512], in_=ptrb[r % 2][:, 0:512]))
        t_astk = t_aev_prev

        bbuf = [n1b[:, 0:1024], n1b[:, 1024:2048]]
        t_bacc = None
        t_ldb_prev = []
        t_baccs = []
        for n in range(NL):
            if n >= 2:
                W("gpsimd", [t_baccs[n - 2]])
            t_ld = DMA("gpsimd", f"lorb{n % 2}", lambda g, n=n, b=bbuf[n % 2]: g.dma_start(
                out=b, in_=lora_b[n:n + 1, :].rearrange("1 (p c) -> p c", p=P)))
            W("vector", [t_ld, tc["gater"], t_rdre, t_ksum_done] + ([t_bacc] if t_bacc else []))
            if n == 0:
                t_bacc = E("vector", lambda v, b=bbuf[0]: v.tensor_scalar(
                    out=xb[0][:, 0:1024], in0=b, scalar1=gr_s[:, 0:1], scalar2=None, op0=OP.mult))
            else:
                t_bacc = E("vector", lambda v, n=n, b=bbuf[n % 2]: v.scalar_tensor_tensor(
                    out=xb[0][:, 0:1024], in0=b, scalar=gr_s[:, n:n + 1], op0=OP.mult, in1=xb[0][:, 0:1024], op1=OP.add))
            t_baccs.append(t_bacc)
            t_ldb_prev.append(t_ld)
        W("scalar", [t_bacc])
        t_bbcast = E("scalar", lambda s: s.activation(out=n1b[:, 2048:3072], in_=xb[0][:, 0:1024], func=AF.Identity))
        t_zf = E("vector", lambda v: v.memset(tra[:, :], 0.0))
        W("gpsimd", [t_zf])
        t_zfd = DMA("gpsimd", "lzf", lambda g: g.dma_start(
            out=bass.AP(bpad_scr_t, 0, [[4096, P], [1, 4096]]), in_=tra[:, :]))
        W("gpsimd", [t_bbcast, t_zfd])
        t_bw = DMA("gpsimd", "lmisc", lambda g: g.dma_start(
            out=bass.AP(bpad_scr_t, 0, [[RANK * P, P], [P, RANK], [1, RANK]]), in_=n1b[:, 2048:3072]))
        W("sync", [t_bw, t_atr_last])
        t_bx = DMA("sync", "lora2", lambda s: s.dma_start(out=trv[:, :], in_=bpad_scr, transpose=True))
        W("vector", [t_bx])
        t_bst1 = E("vector", lambda v: v.tensor_copy(out=bstkT[0:RANK, :], in_=trv[0:RANK, :]))
        W("gpsimd", [t_bacc])
        t_ldbs = DMA("gpsimd", "lbs", lambda g: g.dma_start(
            out=n1b[:, 0:1024], in_=lora_bs[0:1, :].rearrange("1 (p c) -> p c", p=P)))
        W("gpsimd", [t_ldbs, t_bst1])
        t_bw2 = DMA("gpsimd", "lmisc", lambda g: g.dma_start(
            out=bass.AP(bpad_scr_t, 0, [[RANK * P, P], [P, RANK], [1, RANK]]), in_=n1b[:, 0:1024]))
        W("sync", [t_bw2])
        t_bx2 = DMA("sync", "lora2", lambda s: s.dma_start(out=trv[:, :], in_=bpad_scr, transpose=True))
        W("vector", [t_bx2])
        t_bst2 = E("vector", lambda v: v.tensor_copy(out=bstkT[RANK:2 * RANK, :], in_=trv[0:RANK, :]))

        W("scalar", tsm + [t_bbcast])
        t_bh = E("scalar", lambda s: s.activation(out=bhi[:, :], in_=b128[:, :], func=AF.Identity))
        W("vector", [t_bh])
        t_bhf = E("vector", lambda v: v.tensor_copy(out=bhif[:, :], in_=bhi[:, :]))
        W("vector", [t_bhf])
        t_bl = E("vector", lambda v: v.tensor_tensor(out=blo[:, :], in0=b128[:, :], in1=bhif[:, :], op=OP.subtract))
        W("gpsimd", [t_bh, t_bl])
        t_bs1 = DMA("gpsimd", "lbias", lambda g: g.dma_start(
            out=bias_scr[0:1, :].rearrange("1 (p c) -> p c", p=P), in_=bhi[:, :]))
        t_bs2 = DMA("gpsimd", "lbias", lambda g: g.dma_start(
            out=bias_scr[1:2, :].rearrange("1 (p c) -> p c", p=P), in_=blo[:, :]))
        W("sync", [t_bs1, t_bs2])
        t_bhl = DMA("sync", "lora2", lambda s: s.dma_start(out=bhl[:, :], in_=bias_scr))

        def ln_finalize(st, n_elems, t_s1tok, t_q1tok):
            W("vector", [t_s1tok, t_q1tok, pr.now("scalar")])
            a = E("vector", lambda v: v.tensor_scalar(out=st["mean"][:, :], in0=st["s1"][:, :], scalar1=1.0 / n_elems, scalar2=None, op0=OP.mult))
            b = E("vector", lambda v: v.tensor_scalar(out=st["var"][:, :], in0=st["q1"][:, :], scalar1=1.0 / n_elems, scalar2=EPS, op0=OP.mult, op1=OP.add))
            W("vector", [a, b])
            c = E("vector", lambda v: v.tensor_scalar(out=st["rstd"][:, :], in0=st["mean"][:, :], scalar1=st["mean"][:, :], scalar2=-1.0, op0=OP.mult, op1=OP.mult))
            W("vector", [c])
            d = E("vector", lambda v: v.tensor_tensor(out=st["var"][:, :], in0=st["var"][:, :], in1=st["rstd"][:, :], op=OP.add))
            W("vector", [d])
            f = E("vector", lambda v: v.reciprocal(out=st["rstd"][:, :], in_=st["var"][:, :]))
            W("scalar", [f])
            g = E("scalar", lambda s: s.activation(out=st["rstd"][:, :], in_=st["rstd"][:, :], func=AF.Sqrt))
            W("vector", [g])
            h = E("vector", lambda v: v.tensor_scalar(out=st["nbias"][:, :], in0=st["mean"][:, :], scalar1=st["rstd"][:, :], scalar2=-1.0, op0=OP.mult, op1=OP.mult))
            return g, h

        t_x1store = [None] * TT
        t_uload = [None] * TT
        trv_guard = [t_bst1, t_bst2]
        tra_guard = [t_zfd]
        n1b_guard = [t_bw2]
        xfmt_guard = [t_astk]
        ya_guard = [t_acc, t_cb1, t_cb2, t_acast]
        xb0_guard = [t_bbcast, t_bacc]
        xb1_guard = [t_rdim, t_cb1, t_cb2] + gb_toks
        last_convevac = None
        last_ptrb_evac = None
        last_axmm = None

        for t_i in range(TT):
            xbt = xb[t_i % 2]
            war = [t_x1store[t_i - 2]] if t_i >= 2 else ([t_ksum_done, t_bacc, t_bbcast] if t_i % 2 == 0 else [t_rdim, prev_ev])
            if t_i == 0:
                war += xb0_guard
            if t_i == 1:
                war += xb1_guard + [t_ksum_done]
            W("sync", war)
            t_uload[t_i] = DMA("sync", f"uin{t_i % 2}", lambda s, t_i=t_i, xbt=xbt: s.dma_start(
                out=xbt[:, :], in_=u[P * t_i:P * t_i + P, :]))

            W("vector", [t_uload[t_i]] + (trv_guard if t_i == 0 else []))
            t_rs = E("vector", lambda v, xbt=xbt: v.tensor_scalar(
                out=trv[:, :], in0=xbt[:, :], scalar1=1.0, scalar2=None, op0=OP.mult, op1=OP.add,
                accum_out=st1["s1"][:, :]))
            W("scalar", [t_uload[t_i]] + (tra_guard if t_i == 0 else []))
            t_sq = E("scalar", lambda s, xbt=xbt: s.activation(
                out=tra[:, :], in_=xbt[:, :], func=AF.Square, accum_out=st1["q1"][:, :]))
            t_r1, t_nb1 = ln_finalize(st1, D, t_rs, t_sq)
            W("scalar", [t_nb1, pr.now("tensor")] + ([t_ldbs] if t_i == 0 else []) + (n1b_guard if t_i == 0 else []))
            t_n1 = E("scalar", lambda s, xbt=xbt: s.activation(
                out=n1b[:, :], in_=xbt[:, :], func=AF.Identity, bias=st1["nbias"][:, :], scale=st1["rstd"][:, :]))

            ev_toks = []
            for r in range(4):
                toks = [t_n1]
                if r >= 2:
                    toks.append(ev_toks[r - 2])
                elif last_ptrb_evac is not None:
                    toks.append(last_ptrb_evac)
                if t_i == 0 and r < 2:
                    toks += [t_astk, t_kmbT]
                    toks += xfmt_guard
                W("tensor", toks)
                tlast = None
                for j in range(8):
                    c = 8 * r + j
                    tlast = E("tensor", lambda t, c=c, r=r, j=j: t.transpose(
                        out=ptrb[r % 2][:, 128 * j:128 * j + 128], in_=n1b[:, 128 * c:128 * c + 128],
                        identity=idsb[:, :]))
                W("vector", [tlast])
                ev_toks.append(E("vector", lambda v, r=r: v.tensor_copy(
                    out=xfmt[:, 1024 * r:1024 * r + 1024], in_=ptrb[r % 2][:, :])))
            last_ptrb_evac = ev_toks[-1]

            conv_ev = [None] * 8
            for g in range(8):
                toks = [ev_toks[-1], t_tband]
                if g >= 3:
                    toks.append(conv_ev[g - 3])
                elif last_convevac is not None:
                    toks.append(last_convevac)
                elif t_i == 0:
                    toks += [t_ksum_done, gb_toks[-1]]
                W("tensor", toks)
                tg = None
                for ij in range(4 * g + 4):
                    j0 = 512 * g - 128 * ij + 384
                    tg = E("tensor", lambda t, g=g, ij=ij, j0=j0: t.matmul(
                        out=pconv[g % 3][:, :], lhsT=xfmt[:, 128 * ij:128 * ij + 128],
                        rhs=tband[:, j0:j0 + 512], start=(ij == 0), stop=(ij == 4 * g + 3)))
                W("vector", [tg] + ([pr.now("scalar")] + (ya_guard if t_i == 0 else []) if g == 0 else []))
                conv_ev[g] = E("vector", lambda v, g=g: v.scalar_tensor_tensor(
                    out=ya[:, 512 * g:512 * g + 512], in0=n1b[:, 512 * g:512 * g + 512],
                    scalar=dbc[:, :], op0=OP.mult, in1=pconv[g % 3][:, :], op1=OP.add,
                    accum_out=yacols[:, g:g + 1]))
                W("scalar", [conv_ev[g]])
                E("scalar", lambda s, g=g: s.activation(
                    out=tra[:, 512 * g:512 * g + 512], in_=ya[:, 512 * g:512 * g + 512],
                    func=AF.Square, accum_out=yaqcols[:, g:g + 1]))
            last_convevac = conv_ev[7]
            t_sqa = pr.now("scalar")

            W("vector", [conv_ev[7]])
            t_ys = E("vector", lambda v: v.tensor_scalar(
                out=trv[:, 0:8], in0=yacols[:, :], scalar1=1.0, scalar2=None, op0=OP.mult, op1=OP.add,
                accum_out=sta["s1"][:, :]))
            W("vector", [t_sqa])
            t_yq = E("vector", lambda v: v.tensor_scalar(
                out=trv[:, 8:16], in0=yaqcols[:, :], scalar1=1.0, scalar2=None, op0=OP.mult, op1=OP.add,
                accum_out=sta["q1"][:, :]))
            t_ra, t_nba = ln_finalize(sta, D, t_ys, t_yq)
            W("vector", [t_ra, t_nba])
            t_nsv = E("vector", lambda v, t_i=t_i: v.tensor_copy(out=nall[:, t_i:t_i + 1], in_=sta["nbias"][:, :]))
            t_x1 = E("vector", lambda v, xbt=xbt: v.scalar_tensor_tensor(
                out=xbt[:, :], in0=ya[:, :], scalar=sta["rstd"][:, :], op0=OP.mult,
                in1=xbt[:, :], op1=OP.add))
            W("sync", [t_x1])
            t_x1store[t_i] = DMA("sync", f"x1o{t_i % 2}", lambda s, t_i=t_i, xbt=xbt: s.dma_start(
                out=x1_scr[P * t_i:P * t_i + P, :], in_=xbt[:, :]))

            W("vector", [t_x1])
            t_rs2 = E("vector", lambda v, xbt=xbt: v.tensor_scalar(
                out=trv[:, :], in0=xbt[:, :], scalar1=1.0, scalar2=None, op0=OP.mult, op1=OP.add,
                accum_out=st2["s1"][:, :]))
            W("scalar", [t_x1])
            t_sq2 = E("scalar", lambda s, xbt=xbt: s.activation(
                out=tra[:, :], in_=xbt[:, :], func=AF.Square, accum_out=st2["q1"][:, :]))
            t_r2, t_nb2 = ln_finalize(st2, D, t_rs2, t_sq2)
            W("scalar", [t_nb2, conv_ev[7]])
            t_n2 = E("scalar", lambda s, xbt=xbt: s.activation(
                out=n1b[:, :], in_=xbt[:, :], func=AF.Identity, bias=st2["nbias"][:, :], scale=st2["rstd"][:, :]))

            for r in range(4):
                toks = [t_n2, ev_toks[(r + 2) % 4]]
                W("tensor", toks)
                tlast = None
                for j in range(8):
                    c = 8 * r + j
                    tlast = E("tensor", lambda t, c=c, r=r, j=j: t.transpose(
                        out=ptrb[r % 2][:, 128 * j:128 * j + 128], in_=n1b[:, 128 * c:128 * c + 128],
                        identity=idsb[:, :]))
                W("vector", [tlast])
                t_ev2 = E("vector", lambda v, r=r, t_i=t_i: v.tensor_copy(
                    out=bass.AP(n2fm, TOK * 8 * r + P * t_i, [[FC * TOK, P], [TOK, 8], [1, P]]),
                    in_=ptrb[r % 2][:, :]))
                ev_toks[r] = t_ev2
                W("tensor", [t_ev2, t_astk])
                for j in range(8):
                    c = 8 * r + j
                    last_axmm = E("tensor", lambda t, c=c, t_i=t_i: t.matmul(
                        out=pax[:, P * t_i:P * t_i + P], lhsT=astkT[:, NS * c:NS * c + NS],
                        rhs=bass.AP(n2fm, TOK * c + P * t_i, [[FC * TOK, P], [1, P]]),
                        start=(c == 0), stop=(c == FC - 1)))
            last_ptrb_evac = ev_toks[3]

        W("vector", [last_axmm])
        t_axb = E("vector", lambda v: v.tensor_copy(out=axb[:, :], in_=pax[:, :]))

        for cm in reversed(ps_cms):
            cm.__exit__(None, None, None)
        ps_cms2 = []

        def psum_open2(name, shape, dt):
            cm = nc.psum_tensor(name, shape, dt)
            ps_cms2.append(cm)
            return cm.__enter__()

        pffn = [psum_open2(f"pffn{i}", [P, 512], F32) for i in range(8)]
        ps1_clear = [t_axb, last_convevac, last_ptrb_evac, t_kmbT]

        t_wload = {}
        t_fev = {}
        t_fsq = {}
        t_fst = {}
        t_ffmm = {}
        t_p1v = pr.now("vector")
        t_p1a = pr.now("scalar")
        t_yst = [None] * TT
        kglob = 0
        for oc in range(8):
            W("tensor", [t_bhl, t_axb, tc["ones2b"]] + ps1_clear)
            for t2 in range(TT):
                if oc > 0:
                    W("tensor", [t_fev[(oc - 1, t2)]])
                E("tensor", lambda t, t2=t2, oc=oc: t.matmul(
                    out=pffn[t2][:, :], lhsT=ones2bs[:, :], rhs=bhl[:, 512 * oc:512 * oc + 512],
                    start=True, stop=False))
            for f in range(FC):
                kg = kglob
                war = [t_ffmm[kg - 4]] if kg >= 4 else []
                W("gpsimd", war)
                t_wload[kg] = DMA("gpsimd", f"wload{kg % 4}", lambda g, f=f, oc=oc, kg=kg: g.dma_start(
                    out=wbuf[kg % 4][:, :], in_=wk[128 * f:128 * f + 128, 512 * oc:512 * oc + 512]))
                W("tensor", [t_wload[kg]])
                tlast = None
                for t2 in range(TT):
                    tlast = E("tensor", lambda t, f=f, t2=t2, kg=kg: t.matmul(
                        out=pffn[t2][:, :], lhsT=bass.AP(n2fm, TOK * f + P * t2, [[FC * TOK, P], [1, P]]),
                        rhs=wbuf[kg % 4][:, :], start=False, stop=False))
                t_ffmm[kg] = tlast
                kglob += 1
            for t2 in range(TT):
                t_dl = E("tensor", lambda t, t2=t2, oc=oc: t.matmul(
                    out=pffn[t2][:, :], lhsT=axb[:, P * t2:P * t2 + P], rhs=bstkT[:, 512 * oc:512 * oc + 512],
                    start=False, stop=True))
                idx = oc * TT + t2
                sbuf_i = idx % 4
                war = []
                if idx >= 4:
                    rprev = (idx - 4) % TT
                    rec = pr.dsems[f"frow{rprev}"]
                    war = [(rec[0], rec[1]), t_fsq[idx - 4]]
                W("vector", [t_dl] + war)
                t_fev[(oc, t2)] = E("vector", lambda v, t2=t2, oc=oc, sbuf_i=sbuf_i: v.tensor_scalar(
                    out=fstage[sbuf_i][:, :], in0=pffn[t2][:, :], scalar1=1.0, scalar2=None,
                    op0=OP.mult, op1=OP.add, accum_out=fsums[t2][:, oc:oc + 1]))
                W("scalar", [t_fev[(oc, t2)]])
                t_fsq[idx] = E("scalar", lambda s, sbuf_i=sbuf_i, t2=t2, oc=oc: s.activation(
                    out=tra[:, 512 * sbuf_i:512 * sbuf_i + 512], in_=fstage[sbuf_i][:, :],
                    func=AF.Square, accum_out=fsqs[t2][:, oc:oc + 1]))
                W("sync", [t_fev[(oc, t2)]])
                t_fst[idx] = DMA("sync", f"frow{t2}", lambda s, t2=t2, oc=oc, sbuf_i=sbuf_i: s.dma_start(
                    out=f_scr[P * t2:P * t2 + P, 512 * oc:512 * oc + 512], in_=fstage[sbuf_i][:, :]))
                if oc == 7:
                    t_i = t2
                    war = [t_p1v, t_p1a, t_x1store[TT - 1], t_x1store[TT - 2]]
                    war += [t_fst[o2 * TT + t_i] for o2 in range(8)]
                    if t_i >= 1:
                        war.append(t_yst[t_i - 1])
                    W("sync", war)
                    t_x1l = DMA("sync", "x1in", lambda s, t_i=t_i: s.dma_start(
                        out=xb[0][:, :], in_=x1_scr[P * t_i:P * t_i + P, :]))
                    t_fl = DMA("sync", "fin", lambda s, t_i=t_i: s.dma_start(
                        out=xb[1][:, :], in_=f_scr[P * t_i:P * t_i + P, :]))
                    W("vector", [t_fev[(7, t_i)]])
                    t_fs = E("vector", lambda v, t_i=t_i: v.tensor_scalar(
                        out=trv[:, 0:8], in0=fsums[t_i][:, :], scalar1=1.0, scalar2=None, op0=OP.mult, op1=OP.add,
                        accum_out=stf["s1"][:, :]))
                    W("vector", [t_fsq[7 * TT + t_i]])
                    t_fq = E("vector", lambda v, t_i=t_i: v.tensor_scalar(
                        out=trv[:, 8:16], in0=fsqs[t_i][:, :], scalar1=1.0, scalar2=None, op0=OP.mult, op1=OP.add,
                        accum_out=stf["q1"][:, :]))
                    t_rf, t_nbf = ln_finalize(stf, D, t_fs, t_fq)
                    W("scalar", [t_nbf, t_fl])
                    t_lnf = E("scalar", lambda s: s.activation(
                        out=xb[1][:, :], in_=xb[1][:, :], func=AF.Identity, bias=stf["nbias"][:, :], scale=stf["rstd"][:, :]))
                    W("vector", [t_lnf, t_x1l] + ([t_yst[t_i - 1]] if t_i >= 1 else []))
                    t_add = E("vector", lambda v, t_i=t_i: v.scalar_tensor_tensor(
                        out=ya[:, :], in0=xb[0][:, :], scalar=nall[:, t_i:t_i + 1], op0=OP.add,
                        in1=xb[1][:, :], op1=OP.add))
                    W("sync", [t_add])
                    t_yst[t_i] = DMA("sync", "yout", lambda s, t_i=t_i: s.dma_start(
                        out=y_out[P * t_i:P * t_i + P, :], in_=ya[:, :]))
        W("sync", [t_yst[TT - 1]])

        for cm in reversed(ps_cms2):
            cm.__exit__(None, None, None)

        block = e(nc.Block())
        pr.build(block)

    return nc


def make_in_map(core, inputs, t_tiles=8):
    C = host_constants()
    TOK = t_tiles * P
    u = np.asarray(inputs["L0_attn_u"]).reshape(N_CORES * 1024, D)
    gate = np.asarray(inputs["L0_ffn_gate"])
    grow = gate[core]
    gateq = np.empty((P, 4), np.float32)
    for k in range(4):
        for p in range(P):
            gateq[p, k] = grow[4 * k + p // 32]
    gater = np.broadcast_to(grow[None, :], (P, NL)).astype(np.float32).copy()
    m = {
        "u": np.ascontiguousarray(u[core * 1024: core * 1024 + TOK]),
        "lam_ri": np.asarray(inputs["L0_attn_Lambda_ri"], np.float32),
        "p_ri": np.asarray(inputs["L0_attn_p_ri"], np.float32),
        "q_ri": np.asarray(inputs["L0_attn_q_ri"], np.float32),
        "vc_ri": np.asarray(inputs["L0_attn_Vc_ri"], np.float32).reshape(NS, NS * 2),
        "ct_ri": np.asarray(inputs["L0_attn_Ct_ri"], np.float32).reshape(NS, 2),
        "bp": np.asarray(inputs["L0_attn_Bp"], np.float32).reshape(1, NS),
        "dsc": np.asarray(inputs["L0_attn_D"], np.float32).reshape(1, 1),
        "lstep": np.asarray(inputs["L0_attn_log_step"], np.float32).reshape(1, 1),
        "wk": np.asarray(inputs["L0_ffn_kernel"], np.float32),
        "bias": np.asarray(inputs["L0_ffn_bias"], np.float32).reshape(1, D),
        "lora_a": np.asarray(inputs["L0_ffn_lora_A"], np.float32).reshape(NL * RANK, D),
        "lora_b": np.asarray(inputs["L0_ffn_lora_B"], np.float32).reshape(NL, D * RANK),
        "lora_as": np.asarray(inputs["L0_ffn_lora_A_shared"], np.float32),
        "lora_bs": np.asarray(inputs["L0_ffn_lora_B_shared"], np.float32).reshape(1, D * RANK),
        "gateq": gateq,
        "gater": gater,
    }
    for name, arr in C.items():
        m[name] = arr
    return m


def kernel(**inputs):
    nc = build_nc(8)
    in_maps = [make_in_map(c, inputs) for c in range(N_CORES)]
    res = run_bass_kernel_spmd(nc, in_maps, core_ids=list(range(N_CORES)))
    out = np.concatenate([np.asarray(res.results[c]["y"]) for c in range(N_CORES)], axis=0)
    return out.reshape(1, N_CORES * 1024, D).astype(np.float32)


# version 21
# speedup vs baseline: 1.1308x; 1.0214x over previous
from contextlib import ExitStack
import math
import numpy as np
import ml_dtypes
import concourse.bass as bass
import concourse.mybir as mybir
from concourse.bass_utils import run_bass_kernel_spmd

F32 = mybir.dt.float32
BF16 = mybir.dt.bfloat16
AF = mybir.ActivationFunctionType
OP = mybir.AluOpType
AX = mybir.AxisListType.X

P = 128
D = 4096
FC = D // P
NS = 64
RANK = 32
NL = 16
EPS = 1e-5
N_CORES = 8


class Prog:
    ENGS = ["sync", "gpsimd", "vector", "scalar", "tensor"]

    def __init__(self, nc, ctx):
        self.nc = nc
        self.ctx = ctx
        self.ops = {en: [] for en in self.ENGS}
        self.esem = {en: ctx.enter_context(nc.semaphore("tok_" + en))
                     for en in ["vector", "scalar", "tensor"]}
        self.en_cnt = {en: 0 for en in ["vector", "scalar", "tensor"]}
        self.dsems = {}

    def emit(self, en, fn):
        sem = self.esem[en]
        if en in ("vector", "scalar") and self.en_cnt[en] > 0:
            v = self.en_cnt[en]
            self.ops[en].append(lambda e, sem=sem, v=v: e.wait_ge(sem, v))
        self.en_cnt[en] += 1

        def wrapped(e, fn=fn, sem=sem):
            fn(e).then_inc(sem, 1)
        self.ops[en].append(wrapped)
        return (sem, self.en_cnt[en])

    def now(self, en):
        return (self.esem[en], self.en_cnt[en])

    def dma(self, en, stream, fn):
        if stream not in self.dsems:
            self.dsems[stream] = [self.ctx.enter_context(self.nc.semaphore("dma_" + stream)), 0]
        rec = self.dsems[stream]
        sem = rec[0]
        rec[1] += 16

        def wrapped(e, fn=fn, sem=sem):
            fn(e).then_inc(sem, 16)
        self.ops[en].append(wrapped)
        return (sem, rec[1])

    def waits(self, en, toks):
        best = {}
        for t in toks:
            if t is None:
                continue
            sem, v = t
            k = id(sem)
            if k not in best or best[k][1] < v:
                best[k] = (sem, v)
        for sem, v in best.values():
            self.ops[en].append(lambda e, sem=sem, v=v: e.wait_ge(sem, v))

    def build(self, block):
        for en in self.ENGS:
            section = getattr(block, en)
            ops = self.ops[en]

            def make(ops):
                def _(e):
                    for fn in ops:
                        fn(e)
                return _
            section(make(ops))


_CONSTS = None


def host_constants():
    global _CONSTS
    if _CONSTS is not None:
        return _CONSTS
    L = D
    lv = (np.arange(L, dtype=np.float32) / np.float32(L)).astype(np.complex64)
    Omega = np.exp(np.complex64(-2j * np.pi) * lv)
    w = ((np.complex64(1) - Omega) / (np.complex64(1) + Omega)).astype(np.complex64)
    c2 = (np.complex64(2) / (np.complex64(1) + Omega)).astype(np.complex64)

    l1 = np.arange(64)[:, None]
    r1 = np.arange(64)[None, :]
    th = 2.0 * np.pi * l1 * (r1 + 1) / 64.0
    Fre = np.cos(th).astype(np.float32)
    Fim = (-np.sin(th)).astype(np.float32)
    Fimneg = np.sin(th).astype(np.float32)
    r1c = np.arange(64)[:, None]
    l2 = np.arange(64)[None, :]
    thtw = 2.0 * np.pi * l2 * (r1c + 1) / 4096.0
    TWre = np.cos(thtw).astype(np.float32)
    TWim = (-np.sin(thtw)).astype(np.float32)
    l2c = np.arange(64)[:, None]
    r2 = np.arange(64)[None, :]
    thg = 2.0 * np.pi * l2c * r2 / 64.0
    Gre = (np.cos(thg) / 4096.0).astype(np.float32)
    Gimneg = (np.sin(thg) / 4096.0).astype(np.float32)

    _CONSTS = {
        "identb": np.eye(P, dtype=ml_dtypes.bfloat16),
        "identf": np.eye(P, dtype=np.float32),
        "ones2b": np.ones((2, P), dtype=ml_dtypes.bfloat16),
        "ones1f": np.ones((1, P), dtype=np.float32),
        "ones64f": np.ones((1, NS), dtype=np.float32),
        "whre": np.ascontiguousarray(w.real.reshape(1, L)).astype(np.float32),
        "whim": np.ascontiguousarray(w.imag.reshape(1, L)).astype(np.float32),
        "c2re": np.ascontiguousarray(c2.real.reshape(P, 32)).astype(np.float32),
        "c2im": np.ascontiguousarray(c2.imag.reshape(P, 32)).astype(np.float32),
        "Fre": Fre, "Fim": Fim, "Fimneg": Fimneg,
        "TWre": TWre, "TWim": TWim,
        "Gre": Gre, "Gimneg": Gimneg,
    }
    return _CONSTS


def build_nc(t_tiles=8):
    nc = bass.Bass("TRN2", target_bir_lowering=False)
    TT = t_tiles
    TOK = TT * P

    u = nc.dram_tensor("u", [TOK, D], F32, kind="ExternalInput").ap()
    lam_ri = nc.dram_tensor("lam_ri", [NS, 2], F32, kind="ExternalInput").ap()
    p_ri = nc.dram_tensor("p_ri", [NS, 2], F32, kind="ExternalInput").ap()
    q_ri = nc.dram_tensor("q_ri", [NS, 2], F32, kind="ExternalInput").ap()
    vc_ri = nc.dram_tensor("vc_ri", [NS, NS * 2], F32, kind="ExternalInput").ap()
    ct_ri = nc.dram_tensor("ct_ri", [NS, 2], F32, kind="ExternalInput").ap()
    bp_in = nc.dram_tensor("bp", [1, NS], F32, kind="ExternalInput").ap()
    dsc = nc.dram_tensor("dsc", [1, 1], F32, kind="ExternalInput").ap()
    lstep = nc.dram_tensor("lstep", [1, 1], F32, kind="ExternalInput").ap()
    wk = nc.dram_tensor("wk", [D, D], F32, kind="ExternalInput").ap()
    bias_in = nc.dram_tensor("bias", [1, D], F32, kind="ExternalInput").ap()
    lora_a = nc.dram_tensor("lora_a", [NL * RANK, D], F32, kind="ExternalInput").ap()
    lora_b = nc.dram_tensor("lora_b", [NL, D * RANK], F32, kind="ExternalInput").ap()
    lora_as = nc.dram_tensor("lora_as", [RANK, D], F32, kind="ExternalInput").ap()
    lora_bs = nc.dram_tensor("lora_bs", [1, D * RANK], F32, kind="ExternalInput").ap()
    gateq = nc.dram_tensor("gateq", [P, 4], F32, kind="ExternalInput").ap()
    gater = nc.dram_tensor("gater", [P, NL], F32, kind="ExternalInput").ap()

    C = host_constants()
    cin = {}
    for name, arr in C.items():
        dt = BF16 if arr.dtype == ml_dtypes.bfloat16 else F32
        cin[name] = nc.dram_tensor(name, list(arr.shape), dt, kind="ExternalInput").ap()

    y_out = nc.dram_tensor("y", [TOK, D], F32, kind="ExternalOutput").ap()

    x1_scr = nc.dram_tensor("x1_scr", [TOK, D], F32).ap()
    f_scr = nc.dram_tensor("f_scr", [TOK, D], F32).ap()
    krev_t = nc.dram_tensor("krev_scr", [1, 5120], BF16)
    ar_scr = nc.dram_tensor("ar_scr", [2, D], F32).ap()
    k_scr = nc.dram_tensor("k_scr", [16, D], F32).ap()
    bias_scr = nc.dram_tensor("bias_scr", [2, D], BF16).ap()
    bpad_scr_t = nc.dram_tensor("bpad_scr", [D, P], BF16)
    bpad_scr = bpad_scr_t.ap()

    with ExitStack() as ctx:
        e = ctx.enter_context

        xb = [e(nc.sbuf_tensor("xb0", [P, D], F32)), e(nc.sbuf_tensor("xb1", [P, D], F32))]
        ya = e(nc.sbuf_tensor("ya", [P, D], F32))
        n1b = e(nc.sbuf_tensor("n1b", [P, D], BF16))
        xfmt = e(nc.sbuf_tensor("xfmt", [P, D], BF16))
        trv = e(nc.sbuf_tensor("trv", [P, D], BF16))
        tra = e(nc.sbuf_tensor("tra", [P, D], BF16))
        tband = e(nc.sbuf_tensor("tband", [P, 4480], BF16))
        n2fm = e(nc.sbuf_tensor("n2fm", [P, FC * TOK], BF16))
        if FC * TOK >= 4480:
            tbandr = n2fm
        else:
            tbandr = e(nc.sbuf_tensor("tbandr", [P, 4480], BF16))
        astkT = e(nc.sbuf_tensor("astkT", [P, FC * NS], BF16))
        bstkT = e(nc.sbuf_tensor("bstkT", [2 * RANK, D], BF16))
        axb = e(nc.sbuf_tensor("axb", [2 * RANK, TOK], BF16))
        bhl = e(nc.sbuf_tensor("bhl", [2, D], BF16))
        wbuf = [e(nc.sbuf_tensor(f"wbuf{i}", [P, 512], BF16)) for i in range(4)]
        fstage = [e(nc.sbuf_tensor(f"fstage{i}", [P, 512], F32)) for i in range(4)]

        idsb = e(nc.sbuf_tensor("idsb", [P, P], BF16))
        idsf = e(nc.sbuf_tensor("idsf", [P, P], F32))
        ones2bs = e(nc.sbuf_tensor("ones2bs", [2, P], BF16))
        ones1s = e(nc.sbuf_tensor("ones1s", [1, P], F32))
        ones64s = e(nc.sbuf_tensor("ones64s", [1, NS], F32))
        lam_s = e(nc.sbuf_tensor("lam_s", [NS, 2], F32))
        pq_s = e(nc.sbuf_tensor("pq_s", [NS, 4], F32))
        ct_s = e(nc.sbuf_tensor("ct_s", [NS, 2], F32))
        vc_s = e(nc.sbuf_tensor("vc_s", [NS, NS * 2], F32))
        bp_s = e(nc.sbuf_tensor("bp_s", [1, NS], F32))
        bpb = e(nc.sbuf_tensor("bpb", [NS, NS], F32))
        bvec = e(nc.sbuf_tensor("bvec", [NS, 2], F32))
        vmat = e(nc.sbuf_tensor("vmat", [NS, 8], F32))
        tmp641 = e(nc.sbuf_tensor("tmp641", [NS, NS], F32))
        sstep = e(nc.sbuf_tensor("sstep", [1, 1], F32))
        lnbias = e(nc.sbuf_tensor("lnbias", [NS, 1], F32))
        stepb = e(nc.sbuf_tensor("stepb", [NS, 1], F32))
        dbc = e(nc.sbuf_tensor("dbc", [P, 1], F32))
        dtile = e(nc.sbuf_tensor("dtile", [1, 1], F32))
        gq_s = e(nc.sbuf_tensor("gq_s", [P, 4], F32))
        gr_s = e(nc.sbuf_tensor("gr_s", [P, NL], F32))

        kt = [e(nc.sbuf_tensor(f"kt{i}", [P, 32], F32)) for i in range(16)]
        cc = [e(nc.sbuf_tensor(f"cc{i}", [P, 32], F32)) for i in range(6)]
        arre = e(nc.sbuf_tensor("arre", [P, 32], F32))
        arim = e(nc.sbuf_tensor("arim", [P, 32], F32))
        c2re_s = e(nc.sbuf_tensor("c2re_s", [P, 32], F32))
        c2im_s = e(nc.sbuf_tensor("c2im_s", [P, 32], F32))

        fft_s = {nm: e(nc.sbuf_tensor(nm + "_s", [NS, NS], F32))
                 for nm in ["Fre", "Fim", "Fimneg", "TWre", "TWim", "Gre", "Gimneg"]}
        arm_re = e(nc.sbuf_tensor("arm_re", [NS, NS], F32))
        arm_im = e(nc.sbuf_tensor("arm_im", [NS, NS], F32))
        Are = e(nc.sbuf_tensor("Are", [NS, NS], F32))
        Aim = e(nc.sbuf_tensor("Aim", [NS, NS], F32))
        Bre = e(nc.sbuf_tensor("BreT", [NS, NS], F32))
        Bim = e(nc.sbuf_tensor("BimT", [NS, NS], F32))
        Btre = e(nc.sbuf_tensor("Btre", [NS, NS], F32))
        Btim = e(nc.sbuf_tensor("Btim", [NS, NS], F32))
        kmatb = e(nc.sbuf_tensor("kmatb", [NS, NS], BF16))
        kmatbT = e(nc.sbuf_tensor("kmatbT", [NS, NS], BF16))
        wh_s = [e(nc.sbuf_tensor(f"wh_s{i}", [1, 512], F32)) for i in range(2)]
        b128 = e(nc.sbuf_tensor("b128", [P, 32], F32))
        bhi = e(nc.sbuf_tensor("bhi", [P, 32], BF16))
        bhif = e(nc.sbuf_tensor("bhif", [P, 32], F32))
        blo = e(nc.sbuf_tensor("blo", [P, 32], BF16))

        def stat_site(tag):
            return {nm: e(nc.sbuf_tensor(f"{tag}_{nm}", [P, 1], F32))
                    for nm in ["s1", "q1", "mean", "var", "rstd", "nbias"]}
        st1 = stat_site("st1")
        sta = stat_site("sta")
        st2 = stat_site("st2")
        stf = stat_site("stf")
        nall = e(nc.sbuf_tensor("nall", [P, 8], F32))
        yacols = e(nc.sbuf_tensor("yacols", [P, 8], F32))
        yaqcols = e(nc.sbuf_tensor("yaqcols", [P, 8], F32))
        fsums = [e(nc.sbuf_tensor(f"fsums{t}", [P, 8], F32)) for t in range(TT)]
        fsqs = [e(nc.sbuf_tensor(f"fsqs{t}", [P, 8], F32)) for t in range(TT)]

        pr = Prog(nc, ctx)
        E = pr.emit
        W = pr.waits
        DMA = pr.dma

        ps_cms = []

        def psum_open(name, shape, dt):
            cm = nc.psum_tensor(name, shape, dt)
            ps_cms.append(cm)
            return cm.__enter__()

        putil = psum_open("putil", [P, 512], F32)
        pconv = [psum_open(f"pconv{i}", [P, 512], F32) for i in range(3)]
        ptrb = [psum_open(f"ptrb{i}", [P, 1024], BF16) for i in range(2)]
        pax = psum_open("pax", [2 * RANK, TOK], F32)

        tc = {}
        const_loads = [
            ("identb", idsb), ("identf", idsf), ("ones2b", ones2bs),
            ("ones1f", ones1s), ("ones64f", ones64s),
            ("c2re", c2re_s), ("c2im", c2im_s),
        ]
        for name, dst in const_loads:
            tc[name] = DMA("sync", "consts", lambda s, d=dst, src=cin[name]: s.dma_start(out=d[:, :], in_=src))
        for nm in fft_s:
            tc[nm] = DMA("sync", "consts", lambda s, d=fft_s[nm], src=cin[nm]: s.dma_start(out=d[:, :], in_=src))
        tc["gateq"] = DMA("sync", "consts", lambda s: s.dma_start(out=gq_s[:, :], in_=gateq))
        tc["gater"] = DMA("sync", "consts", lambda s: s.dma_start(out=gr_s[:, :], in_=gater))

        tsm = []
        for src, dst in [(lam_ri, lam_s[:, :]), (ct_ri, ct_s[:, :]), (vc_ri, vc_s[:, :]),
                         (bp_in, bp_s[:, :]), (dsc, dtile[:, :]), (lstep, sstep[:, :])]:
            tsm.append(DMA("sync", "smalls", lambda s, src=src, dst=dst: s.dma_start(out=dst, in_=src)))
        tsm.append(DMA("sync", "smalls", lambda s: s.dma_start(out=pq_s[:, 0:2], in_=p_ri)))
        tsm.append(DMA("sync", "smalls", lambda s: s.dma_start(out=pq_s[:, 2:4], in_=q_ri)))
        tsm.append(DMA("sync", "smalls", lambda s: s.dma_start(
            out=b128[:, :], in_=bias_in.rearrange("1 (p c) -> p c", p=P))))
        t_call = (pr.dsems["consts"][0], pr.dsems["consts"][1])
        t_sall = (pr.dsems["smalls"][0], pr.dsems["smalls"][1])
        for _k in list(tc):
            tc[_k] = t_call
        tsm = [t_sall]

        W("scalar", tsm)
        t_exp = E("scalar", lambda s: s.activation(out=sstep[:, :], in_=sstep[:, :], func=AF.Exp, scale=-1.0))
        W("vector", [t_exp])
        t_2os = E("vector", lambda v: v.tensor_scalar(out=sstep[:, :], in0=sstep[:, :], scalar1=2.0, scalar2=None, op0=OP.mult))
        W("tensor", [tc["ones64f"], t_2os])
        t_mm1 = E("tensor", lambda t: t.matmul(out=putil[0:NS, 0:1], lhsT=ones64s[:, :], rhs=sstep[:, :], start=True, stop=True))
        W("vector", [t_mm1])
        t_stepb = E("vector", lambda v: v.tensor_copy(out=stepb[:, :], in_=putil[0:NS, 0:1]))

        W("tensor", [tc["ones1f"], t_stepb] + tsm)
        t_mm2 = E("tensor", lambda t: t.matmul(out=putil[:, 1:2], lhsT=ones1s[:, :], rhs=dtile[:, :], start=True, stop=True))
        W("vector", [t_mm2])
        t_dbc = E("vector", lambda v: v.tensor_copy(out=dbc[:, :], in_=putil[:, 1:2]))

        W("tensor", [t_dbc])
        t_mm3 = E("tensor", lambda t: t.matmul(out=putil[0:NS, 0:NS], lhsT=ones64s[:, :], rhs=bp_s[:, :], start=True, stop=True))
        W("vector", [t_mm3])
        t_bpb = E("vector", lambda v: v.tensor_copy(out=bpb[:, :], in_=putil[0:NS, 0:NS]))

        W("vector", [t_bpb])
        t1 = E("vector", lambda v: v.tensor_tensor(out=tmp641[:, :], in0=bass.AP(vc_s, 0, [[NS * 2, NS], [2, NS]]), in1=bpb[:, :], op=OP.mult))
        W("vector", [t1])
        t_bre = E("vector", lambda v: v.reduce_sum(out=bvec[:, 0:1], in_=tmp641[:, :], axis=AX))
        t2 = E("vector", lambda v: v.tensor_tensor(out=tmp641[:, :], in0=bass.AP(vc_s, 1, [[NS * 2, NS], [2, NS]]), in1=bpb[:, :], op=OP.mult))
        W("vector", [t2])
        t_bim = E("vector", lambda v: v.reduce_sum(out=bvec[:, 1:2], in_=tmp641[:, :], axis=AX))
        W("vector", [t_bre, t_bim])

        def cmul_conj(col, are, aim, bre, bim):
            ta = E("vector", lambda v: v.tensor_tensor(out=vmat[:, 2 * col:2 * col + 1], in0=are, in1=bre, op=OP.mult))
            tb = E("vector", lambda v: v.tensor_tensor(out=tmp641[:, 0:1], in0=aim, in1=bim, op=OP.mult))
            W("vector", [ta, tb])
            E("vector", lambda v: v.tensor_tensor(out=vmat[:, 2 * col:2 * col + 1], in0=vmat[:, 2 * col:2 * col + 1], in1=tmp641[:, 0:1], op=OP.add))
            tcsh = E("vector", lambda v: v.tensor_tensor(out=vmat[:, 2 * col + 1:2 * col + 2], in0=are, in1=bim, op=OP.mult))
            td = E("vector", lambda v: v.tensor_tensor(out=tmp641[:, 1:2], in0=aim, in1=bre, op=OP.mult))
            W("vector", [tcsh, td])
            return E("vector", lambda v: v.tensor_tensor(out=vmat[:, 2 * col + 1:2 * col + 2], in0=vmat[:, 2 * col + 1:2 * col + 2], in1=tmp641[:, 1:2], op=OP.subtract))

        t_v0 = cmul_conj(0, ct_s[:, 0:1], ct_s[:, 1:2], bvec[:, 0:1], bvec[:, 1:2])
        t_v1 = cmul_conj(1, ct_s[:, 0:1], ct_s[:, 1:2], pq_s[:, 0:1], pq_s[:, 1:2])
        t_v2 = cmul_conj(2, pq_s[:, 2:3], pq_s[:, 3:4], bvec[:, 0:1], bvec[:, 1:2])
        t_v3 = cmul_conj(3, pq_s[:, 2:3], pq_s[:, 3:4], pq_s[:, 0:1], pq_s[:, 1:2])
        t_vv = [t_v0, t_v1, t_v2, t_v3]

        gb_toks = []
        gb_mms = []
        prev_sc = None
        pot = [putil, pconv[0], pconv[1]]
        kwh = 0
        for half, dst in [(0, xb[0]), (1, xb[1])]:
            for cchunk in range(8):
                src_c = cin["whre"] if half == 0 else cin["whim"]
                if kwh >= 2:
                    W("sync", [gb_mms[kwh - 2]])
                t_wh = DMA("sync", f"wh{kwh % 2}", lambda s, src_c=src_c, cchunk=cchunk, kwh=kwh: s.dma_start(
                    out=wh_s[kwh % 2][:, :], in_=src_c[0:1, 512 * cchunk:512 * cchunk + 512]))
                toks = [t_wh, t_stepb, t_bpb]
                if kwh >= 3:
                    toks.append(gb_toks[kwh - 3])
                W("tensor", toks)
                tmm = E("tensor", lambda t, kwh=kwh: t.matmul(
                    out=pot[kwh % 3][0:NS, :], lhsT=ones64s[:, :],
                    rhs=wh_s[kwh % 2][:, :], start=True, stop=True))
                gb_mms.append(tmm)
                W("scalar", [tmm])
                prev_sc = E("scalar", lambda s, dst=dst, cchunk=cchunk, kwh=kwh: s.activation(
                    out=dst[0:NS, 512 * cchunk:512 * cchunk + 512], in_=pot[kwh % 3][0:NS, :],
                    func=AF.Identity, scale=stepb[:, :]))
                gb_toks.append(prev_sc)
                kwh += 1

        W("vector", gb_toks + t_vv)
        t_dre = E("vector", lambda v: v.tensor_scalar(out=xb[0][0:NS, :], in0=xb[0][0:NS, :], scalar1=lam_s[:, 0:1], scalar2=None, op0=OP.subtract))
        t_dim = E("vector", lambda v: v.tensor_scalar(out=xb[1][0:NS, :], in0=xb[1][0:NS, :], scalar1=lam_s[:, 1:2], scalar2=None, op0=OP.subtract))
        W("scalar", [t_dre])
        t_s1 = E("scalar", lambda s: s.activation(out=ya[0:NS, :], in_=xb[0][0:NS, :], func=AF.Square))
        W("scalar", [t_dim])
        t_s2 = E("scalar", lambda s: s.activation(out=xb[1][0:NS, :], in_=xb[1][0:NS, :], func=AF.Square))
        W("vector", [t_s1, t_s2])
        t_s12 = E("vector", lambda v: v.tensor_tensor(out=ya[0:NS, :], in0=ya[0:NS, :], in1=xb[1][0:NS, :], op=OP.add))
        W("scalar", [t_s12])
        t_ln = E("scalar", lambda s: s.activation(out=ya[0:NS, :], in_=ya[0:NS, :], func=AF.Ln, scale=float(2.0 ** -32)))
        W("scalar", [t_ln])
        t_lb = E("vector", lambda v: v.memset(lnbias[:, :], float(-32.0 * math.log(2.0))))
        W("scalar", [t_lb])
        t_rec = E("scalar", lambda s: s.activation(out=ya[0:NS, :], in_=ya[0:NS, :], func=AF.Exp, scale=-1.0, bias=lnbias[:, :]))
        W("vector", [t_rec])
        t_rdre = E("vector", lambda v: v.tensor_tensor(out=xb[0][0:NS, :], in0=xb[0][0:NS, :], in1=ya[0:NS, :], op=OP.mult))
        prev_sc2 = t_s12
        for cchunk in range(8):
            if kwh >= 2:
                W("sync", [gb_mms[kwh - 2]])
            t_wh = DMA("sync", f"wh{kwh % 2}", lambda s, cchunk=cchunk, kwh=kwh: s.dma_start(
                out=wh_s[kwh % 2][:, :], in_=cin["whim"][0:1, 512 * cchunk:512 * cchunk + 512]))
            W("tensor", [t_wh, gb_toks[kwh - 3], t_s12 if cchunk == 0 else prev_sc2])
            tmm = E("tensor", lambda t, kwh=kwh: t.matmul(
                out=pot[kwh % 3][0:NS, :], lhsT=ones64s[:, :],
                rhs=wh_s[kwh % 2][:, :], start=True, stop=True))
            gb_mms.append(tmm)
            W("scalar", [tmm])
            prev_sc2 = E("scalar", lambda s, cchunk=cchunk, kwh=kwh: s.activation(
                out=xb[1][0:NS, 512 * cchunk:512 * cchunk + 512], in_=pot[kwh % 3][0:NS, :],
                func=AF.Identity, scale=stepb[:, :]))
            gb_toks.append(prev_sc2)
            kwh += 1
        W("vector", [prev_sc2])
        t_dim2 = E("vector", lambda v: v.tensor_scalar(out=xb[1][0:NS, :], in0=xb[1][0:NS, :], scalar1=lam_s[:, 1:2], scalar2=None, op0=OP.subtract))
        t_rdim = E("vector", lambda v: v.scalar_tensor_tensor(out=xb[1][0:NS, :], in0=xb[1][0:NS, :], scalar=-1.0, op0=OP.mult, in1=ya[0:NS, :], op1=OP.mult))

        prev_ev = None
        ks_evs = []
        kk = 0
        for src, outrow, tok in [(xb[0], 0, t_rdre), (xb[1], 32, t_rdim)]:
            for cchunk in range(8):
                toks = [tok, t_rec, t_rdim]
                if kk >= 3:
                    toks.append(ks_evs[kk - 3])
                else:
                    toks.append(gb_toks[-1])
                W("tensor", toks)
                tmm = E("tensor", lambda t, src=src, cchunk=cchunk, kk=kk: t.matmul(
                    out=pot[kk % 3][0:8, :], lhsT=vmat[:, :], rhs=src[0:NS, 512 * cchunk:512 * cchunk + 512],
                    start=True, stop=True))
                W("vector", [tmm])
                prev_ev = E("vector", lambda v, outrow=outrow, cchunk=cchunk, kk=kk: v.tensor_copy(
                    out=ya[outrow:outrow + 8, 512 * cchunk:512 * cchunk + 512], in_=pot[kk % 3][0:8, :]))
                ks_evs.append(prev_ev)
                kk += 1
        t_ksum_done = prev_ev
        W("gpsimd", [t_ksum_done])
        t_ks1 = DMA("gpsimd", "kscr", lambda g: g.dma_start(out=k_scr[0:8, :], in_=ya[0:8, :]))
        t_ks2 = DMA("gpsimd", "kscr", lambda g: g.dma_start(out=k_scr[8:16, :], in_=ya[32:40, :]))
        W("gpsimd", [t_ks1, t_ks2])
        t_ktl = []
        for i in range(16):
            t_ktl.append(DMA("gpsimd", "kscr", lambda g, i=i: g.dma_start(
                out=kt[i][:, :], in_=k_scr[i:i + 1, :].rearrange("1 (p c) -> p c", p=P))))

        W("vector", t_ktl + [tc["c2re"], tc["c2im"]])
        TTE = lambda fn: E("vector", fn)
        for k in range(4):
            a = TTE(lambda v, k=k: v.tensor_tensor(out=kt[2 * k][:, :], in0=kt[2 * k][:, :], in1=kt[8 + 2 * k + 1][:, :], op=OP.subtract))
            b = TTE(lambda v, k=k: v.tensor_tensor(out=kt[8 + 2 * k][:, :], in0=kt[8 + 2 * k][:, :], in1=kt[2 * k + 1][:, :], op=OP.add))
            W("vector", [a, b])
        t = TTE(lambda v: v.tensor_scalar(out=kt[6][:, :], in0=kt[6][:, :], scalar1=1.0, scalar2=None, op0=OP.add))
        W("vector", [t])
        t = TTE(lambda v: v.tensor_tensor(out=cc[0][:, :], in0=kt[6][:, :], in1=kt[6][:, :], op=OP.mult))
        t2b = TTE(lambda v: v.tensor_tensor(out=cc[1][:, :], in0=kt[14][:, :], in1=kt[14][:, :], op=OP.mult))
        W("vector", [t, t2b])
        t = TTE(lambda v: v.tensor_tensor(out=cc[0][:, :], in0=cc[0][:, :], in1=cc[1][:, :], op=OP.add))
        W("vector", [t])
        t_recd = TTE(lambda v: v.reciprocal(out=cc[0][:, :], in_=cc[0][:, :]))
        a1 = TTE(lambda v: v.tensor_tensor(out=cc[1][:, :], in0=kt[2][:, :], in1=kt[4][:, :], op=OP.mult))
        a2 = TTE(lambda v: v.tensor_tensor(out=cc[2][:, :], in0=kt[10][:, :], in1=kt[12][:, :], op=OP.mult))
        W("vector", [a1, a2])
        t_numre = TTE(lambda v: v.tensor_tensor(out=cc[1][:, :], in0=cc[1][:, :], in1=cc[2][:, :], op=OP.subtract))
        a3 = TTE(lambda v: v.tensor_tensor(out=cc[2][:, :], in0=kt[2][:, :], in1=kt[12][:, :], op=OP.mult))
        a4 = TTE(lambda v: v.tensor_tensor(out=cc[3][:, :], in0=kt[10][:, :], in1=kt[4][:, :], op=OP.mult))
        W("vector", [a3, a4])
        t_numim = TTE(lambda v: v.tensor_tensor(out=cc[2][:, :], in0=cc[2][:, :], in1=cc[3][:, :], op=OP.add))
        W("vector", [t_numre, t_numim, t_recd])
        b1 = TTE(lambda v: v.tensor_tensor(out=cc[3][:, :], in0=cc[1][:, :], in1=kt[6][:, :], op=OP.mult))
        b2 = TTE(lambda v: v.tensor_tensor(out=cc[4][:, :], in0=cc[2][:, :], in1=kt[14][:, :], op=OP.mult))
        W("vector", [b1, b2])
        t = TTE(lambda v: v.tensor_tensor(out=cc[3][:, :], in0=cc[3][:, :], in1=cc[4][:, :], op=OP.add))
        W("vector", [t])
        t_rre = TTE(lambda v: v.tensor_tensor(out=cc[3][:, :], in0=cc[3][:, :], in1=cc[0][:, :], op=OP.mult))
        b3 = TTE(lambda v: v.tensor_tensor(out=cc[4][:, :], in0=cc[2][:, :], in1=kt[6][:, :], op=OP.mult))
        b4 = TTE(lambda v: v.tensor_tensor(out=cc[5][:, :], in0=cc[1][:, :], in1=kt[14][:, :], op=OP.mult))
        W("vector", [b3, b4])
        t = TTE(lambda v: v.tensor_tensor(out=cc[4][:, :], in0=cc[4][:, :], in1=cc[5][:, :], op=OP.subtract))
        W("vector", [t])
        t_rim = TTE(lambda v: v.tensor_tensor(out=cc[4][:, :], in0=cc[4][:, :], in1=cc[0][:, :], op=OP.mult))
        W("vector", [t_rre, t_rim])
        t_dfre = TTE(lambda v: v.tensor_tensor(out=cc[1][:, :], in0=kt[0][:, :], in1=cc[3][:, :], op=OP.subtract))
        t_dfim = TTE(lambda v: v.tensor_tensor(out=cc[2][:, :], in0=kt[8][:, :], in1=cc[4][:, :], op=OP.subtract))
        W("vector", [t_dfre, t_dfim])
        m1 = TTE(lambda v: v.tensor_tensor(out=arre[:, :], in0=c2re_s[:, :], in1=cc[1][:, :], op=OP.mult))
        m2 = TTE(lambda v: v.tensor_tensor(out=cc[5][:, :], in0=c2im_s[:, :], in1=cc[2][:, :], op=OP.mult))
        W("vector", [m1, m2])
        t_arre = TTE(lambda v: v.tensor_tensor(out=arre[:, :], in0=arre[:, :], in1=cc[5][:, :], op=OP.subtract))
        m3 = TTE(lambda v: v.tensor_tensor(out=arim[:, :], in0=c2re_s[:, :], in1=cc[2][:, :], op=OP.mult))
        m4 = TTE(lambda v: v.tensor_tensor(out=cc[5][:, :], in0=c2im_s[:, :], in1=cc[1][:, :], op=OP.mult))
        W("vector", [m3, m4])
        t_arim = TTE(lambda v: v.tensor_tensor(out=arim[:, :], in0=arim[:, :], in1=cc[5][:, :], op=OP.add))

        W("gpsimd", [t_arre, t_arim])
        t_ar1 = DMA("gpsimd", "kscr", lambda g: g.dma_start(
            out=ar_scr[0:1, :].rearrange("1 (p c) -> p c", p=P), in_=arre[:, :]))
        t_ar2 = DMA("gpsimd", "kscr", lambda g: g.dma_start(
            out=ar_scr[1:2, :].rearrange("1 (p c) -> p c", p=P), in_=arim[:, :]))
        W("gpsimd", [t_ar1, t_ar2])
        t_arm1 = DMA("gpsimd", "kscr", lambda g: g.dma_start(
            out=arm_re[:, :], in_=ar_scr[0:1, :].rearrange("1 (p c) -> p c", p=NS)))
        t_arm2 = DMA("gpsimd", "kscr", lambda g: g.dma_start(
            out=arm_im[:, :], in_=ar_scr[1:2, :].rearrange("1 (p c) -> p c", p=NS)))

        W("tensor", [t_arm1, t_arm2, t_ksum_done] + [tc[nm] for nm in fft_s])
        tm1 = E("tensor", lambda t: t.matmul(out=putil[0:NS, 0:NS], lhsT=fft_s["Fre"][:, :], rhs=arm_re[:, :], start=True, stop=False))
        tm2 = E("tensor", lambda t: t.matmul(out=putil[0:NS, 0:NS], lhsT=fft_s["Fimneg"][:, :], rhs=arm_im[:, :], start=False, stop=True))
        W("vector", [tm2])
        t_are = E("vector", lambda v: v.tensor_copy(out=Are[:, :], in_=putil[0:NS, 0:NS]))
        W("tensor", [t_are])
        tm3 = E("tensor", lambda t: t.matmul(out=putil[0:NS, 0:NS], lhsT=fft_s["Fim"][:, :], rhs=arm_re[:, :], start=True, stop=False))
        tm4 = E("tensor", lambda t: t.matmul(out=putil[0:NS, 0:NS], lhsT=fft_s["Fre"][:, :], rhs=arm_im[:, :], start=False, stop=True))
        W("vector", [tm4])
        t_aim = E("vector", lambda v: v.tensor_copy(out=Aim[:, :], in_=putil[0:NS, 0:NS]))
        W("vector", [t_are, t_aim])
        w1 = E("vector", lambda v: v.tensor_tensor(out=Bre[:, :], in0=Are[:, :], in1=fft_s["TWre"][:, :], op=OP.mult))
        w2 = E("vector", lambda v: v.tensor_tensor(out=tmp641[:, :], in0=Aim[:, :], in1=fft_s["TWim"][:, :], op=OP.mult))
        W("vector", [w1, w2])
        t_bre2 = E("vector", lambda v: v.tensor_tensor(out=Bre[:, :], in0=Bre[:, :], in1=tmp641[:, :], op=OP.subtract))
        w3 = E("vector", lambda v: v.tensor_tensor(out=Bim[:, :], in0=Are[:, :], in1=fft_s["TWim"][:, :], op=OP.mult))
        w4 = E("vector", lambda v: v.tensor_tensor(out=tmp641[:, :], in0=Aim[:, :], in1=fft_s["TWre"][:, :], op=OP.mult))
        W("vector", [w3, w4])
        t_bim2 = E("vector", lambda v: v.tensor_tensor(out=Bim[:, :], in0=Bim[:, :], in1=tmp641[:, :], op=OP.add))
        W("tensor", [t_bre2, tc["identf"]])
        tt1 = E("tensor", lambda t: t.transpose(out=putil[0:NS, 0:NS], in_=Bre[:, :], identity=idsf[0:NS, 0:NS]))
        W("vector", [tt1])
        t_btre = E("vector", lambda v: v.tensor_copy(out=Btre[:, :], in_=putil[0:NS, 0:NS]))
        W("tensor", [t_bim2, t_btre])
        tt2 = E("tensor", lambda t: t.transpose(out=putil[0:NS, 0:NS], in_=Bim[:, :], identity=idsf[0:NS, 0:NS]))
        W("vector", [tt2])
        t_btim = E("vector", lambda v: v.tensor_copy(out=Btim[:, :], in_=putil[0:NS, 0:NS]))
        W("tensor", [t_btim])
        tm5 = E("tensor", lambda t: t.matmul(out=putil[0:NS, 0:NS], lhsT=Btre[:, :], rhs=fft_s["Gre"][:, :], start=True, stop=False))
        tm6 = E("tensor", lambda t: t.matmul(out=putil[0:NS, 0:NS], lhsT=Btim[:, :], rhs=fft_s["Gimneg"][:, :], start=False, stop=True))
        W("vector", [tm6])
        t_kmb = E("vector", lambda v: v.tensor_copy(out=kmatb[:, :], in_=putil[0:NS, 0:NS]))
        W("tensor", [t_kmb, tc["identb"]])
        tt3 = E("tensor", lambda t: t.transpose(out=ptrb[0][0:NS, 0:NS], in_=kmatb[:, :], identity=idsb[0:NS, 0:NS]))
        W("vector", [tt3])
        t_kmbT = E("vector", lambda v: v.tensor_copy(out=kmatbT[:, :], in_=ptrb[0][0:NS, 0:NS]))
        ztile = tband
        t_zt = E("vector", lambda v: v.memset(ztile[0:1, 0:512], 0.0))
        W("gpsimd", [t_kmbT, t_zt])
        t_kv1 = DMA("gpsimd", "kscr", lambda g: g.dma_start(out=bass.AP(krev_t, 512, [[NS, NS], [1, NS]]), in_=kmatbT[:, :]))
        t_kv2 = DMA("gpsimd", "kscr", lambda g: g.dma_start(out=bass.AP(krev_t, 0, [[512, 1], [1, 512]]), in_=ztile[0:1, 0:512]))
        t_kv3 = DMA("gpsimd", "kscr", lambda g: g.dma_start(out=bass.AP(krev_t, 4608, [[512, 1], [1, 512]]), in_=ztile[0:1, 0:512]))
        W("gpsimd", [t_kv1, t_kv2, t_kv3])
        _rpitch = FC * TOK if FC * TOK >= 4480 else 4480
        t_tbr = DMA("gpsimd", "kscr", lambda g: g.dma_start(
            out=bass.AP(tbandr, 0, [[_rpitch, P], [1, 4480]]), in_=bass.AP(krev_t, 512, [[1, P], [1, 4480]])))
        W("vector", [t_tbr])
        t_tband = E("vector", lambda v: v.tensor_copy(
            out=tband[:, :], in_=bass.AP(tbandr, 4479, [[_rpitch, P], [-1, 4480]])))

        abuf = [xfmt, trv]
        t_acc = None
        t_lda_prev = []
        t_accs = []
        for k in range(4):
            toks = [t_ks1, t_ks2]
            if k >= 2:
                W("gpsimd", [t_accs[k - 2]])
            t_ld = DMA("gpsimd", f"lora{k % 2}", lambda g, k=k, b=abuf[k % 2]: g.dma_start(
                out=b[:, :], in_=lora_a[128 * k:128 * k + 128, :]))
            W("vector", [t_ld, tc["gateq"]] + toks + ([t_acc] if t_acc else []))
            if k == 0:
                t_acc = E("vector", lambda v, b=abuf[0]: v.tensor_scalar(
                    out=ya[:, :], in0=b[:, :], scalar1=gq_s[:, 0:1], scalar2=None, op0=OP.mult))
            else:
                t_acc = E("vector", lambda v, k=k, b=abuf[k % 2]: v.scalar_tensor_tensor(
                    out=ya[:, :], in0=b[:, :], scalar=gq_s[:, k:k + 1], op0=OP.mult, in1=ya[:, :], op1=OP.add))
            t_accs.append(t_acc)
            t_lda_prev.append(t_ld)
        W("vector", [t_acc])
        t_cp1 = E("vector", lambda v: v.tensor_copy(out=xb[1][0:NS, :], in_=ya[NS:P, :]))
        t_cb1 = E("vector", lambda v: v.tensor_tensor(out=ya[0:NS, :], in0=ya[0:NS, :], in1=xb[1][0:NS, :], op=OP.add))
        t_cp2 = E("vector", lambda v: v.tensor_copy(out=xb[1][0:RANK, :], in_=ya[RANK:NS, :]))
        t_cb2 = E("vector", lambda v: v.tensor_tensor(out=ya[0:RANK, :], in0=ya[0:RANK, :], in1=xb[1][0:RANK, :], op=OP.add))
        W("scalar", [t_cb2])
        t_acast = E("scalar", lambda s: s.activation(out=trv[0:RANK, :], in_=ya[0:RANK, :], func=AF.Identity))
        W("gpsimd", t_lda_prev + [t_acc])
        t_ash = DMA("gpsimd", "lash", lambda g: g.dma_start(out=trv[RANK:NS, :], in_=lora_as))
        t_aev_prev = None
        t_atr_last = None
        for r in range(4):
            toks = [t_acast, t_ash, tc["identb"], t_tband, t_kmbT]
            if t_aev_prev is not None:
                toks.append(t_aev_prev)
            W("tensor", toks)
            for j in range(8):
                c = 8 * r + j
                t_atr_last = E("tensor", lambda t, c=c, r=r, j=j: t.transpose(
                    out=ptrb[r % 2][:, 64 * j:64 * j + 64], in_=trv[0:NS, 128 * c:128 * c + 128],
                    identity=idsb[0:NS, 0:NS]))
            W("vector", [t_atr_last])
            t_aev_prev = E("vector", lambda v, r=r: v.tensor_copy(
                out=astkT[:, 512 * r:512 * r + 512], in_=ptrb[r % 2][:, 0:512]))
        t_astk = t_aev_prev

        bbuf = [n1b[:, 0:1024], n1b[:, 1024:2048]]
        t_bacc = None
        t_ldb_prev = []
        t_baccs = []
        for n in range(NL):
            if n >= 2:
                W("gpsimd", [t_baccs[n - 2]])
            t_ld = DMA("gpsimd", f"lorb{n % 2}", lambda g, n=n, b=bbuf[n % 2]: g.dma_start(
                out=b, in_=lora_b[n:n + 1, :].rearrange("1 (p c) -> p c", p=P)))
            W("vector", [t_ld, tc["gater"], t_rdre, t_ksum_done] + ([t_bacc] if t_bacc else []))
            if n == 0:
                t_bacc = E("vector", lambda v, b=bbuf[0]: v.tensor_scalar(
                    out=xb[0][:, 0:1024], in0=b, scalar1=gr_s[:, 0:1], scalar2=None, op0=OP.mult))
            else:
                t_bacc = E("vector", lambda v, n=n, b=bbuf[n % 2]: v.scalar_tensor_tensor(
                    out=xb[0][:, 0:1024], in0=b, scalar=gr_s[:, n:n + 1], op0=OP.mult, in1=xb[0][:, 0:1024], op1=OP.add))
            t_baccs.append(t_bacc)
            t_ldb_prev.append(t_ld)
        W("scalar", [t_bacc])
        t_bbcast = E("scalar", lambda s: s.activation(out=n1b[:, 2048:3072], in_=xb[0][:, 0:1024], func=AF.Identity))
        t_zf = E("vector", lambda v: v.memset(tra[:, :], 0.0))
        W("gpsimd", [t_zf])
        t_zfd = DMA("gpsimd", "lzf", lambda g: g.dma_start(
            out=bass.AP(bpad_scr_t, 0, [[4096, P], [1, 4096]]), in_=tra[:, :]))
        W("gpsimd", [t_bbcast, t_zfd])
        t_bw = DMA("gpsimd", "lmisc", lambda g: g.dma_start(
            out=bass.AP(bpad_scr_t, 0, [[RANK * P, P], [P, RANK], [1, RANK]]), in_=n1b[:, 2048:3072]))
        W("sync", [t_bw, t_atr_last])
        t_bx = DMA("sync", "lora2", lambda s: s.dma_start(out=trv[:, :], in_=bpad_scr, transpose=True))
        W("vector", [t_bx])
        t_bst1 = E("vector", lambda v: v.tensor_copy(out=bstkT[0:RANK, :], in_=trv[0:RANK, :]))
        W("gpsimd", [t_bacc])
        t_ldbs = DMA("gpsimd", "lbs", lambda g: g.dma_start(
            out=n1b[:, 0:1024], in_=lora_bs[0:1, :].rearrange("1 (p c) -> p c", p=P)))
        W("gpsimd", [t_ldbs, t_bst1])
        t_bw2 = DMA("gpsimd", "lmisc", lambda g: g.dma_start(
            out=bass.AP(bpad_scr_t, 0, [[RANK * P, P], [P, RANK], [1, RANK]]), in_=n1b[:, 0:1024]))
        W("sync", [t_bw2])
        t_bx2 = DMA("sync", "lora2", lambda s: s.dma_start(out=trv[:, :], in_=bpad_scr, transpose=True))
        W("vector", [t_bx2])
        t_bst2 = E("vector", lambda v: v.tensor_copy(out=bstkT[RANK:2 * RANK, :], in_=trv[0:RANK, :]))

        W("scalar", tsm + [t_bbcast])
        t_bh = E("scalar", lambda s: s.activation(out=bhi[:, :], in_=b128[:, :], func=AF.Identity))
        W("vector", [t_bh])
        t_bhf = E("vector", lambda v: v.tensor_copy(out=bhif[:, :], in_=bhi[:, :]))
        W("vector", [t_bhf])
        t_bl = E("vector", lambda v: v.tensor_tensor(out=blo[:, :], in0=b128[:, :], in1=bhif[:, :], op=OP.subtract))
        W("gpsimd", [t_bh, t_bl])
        t_bs1 = DMA("gpsimd", "lbias", lambda g: g.dma_start(
            out=bias_scr[0:1, :].rearrange("1 (p c) -> p c", p=P), in_=bhi[:, :]))
        t_bs2 = DMA("gpsimd", "lbias", lambda g: g.dma_start(
            out=bias_scr[1:2, :].rearrange("1 (p c) -> p c", p=P), in_=blo[:, :]))
        W("sync", [t_bs1, t_bs2])
        t_bhl = DMA("sync", "lora2", lambda s: s.dma_start(out=bhl[:, :], in_=bias_scr))

        def ln_finalize(st, n_elems, t_s1tok, t_q1tok):
            W("vector", [t_s1tok, t_q1tok, pr.now("scalar")])
            a = E("vector", lambda v: v.tensor_scalar(out=st["mean"][:, :], in0=st["s1"][:, :], scalar1=1.0 / n_elems, scalar2=None, op0=OP.mult))
            b = E("vector", lambda v: v.tensor_scalar(out=st["var"][:, :], in0=st["q1"][:, :], scalar1=1.0 / n_elems, scalar2=EPS, op0=OP.mult, op1=OP.add))
            W("vector", [a, b])
            c = E("vector", lambda v: v.tensor_scalar(out=st["rstd"][:, :], in0=st["mean"][:, :], scalar1=st["mean"][:, :], scalar2=-1.0, op0=OP.mult, op1=OP.mult))
            W("vector", [c])
            d = E("vector", lambda v: v.tensor_tensor(out=st["var"][:, :], in0=st["var"][:, :], in1=st["rstd"][:, :], op=OP.add))
            W("vector", [d])
            f = E("vector", lambda v: v.reciprocal(out=st["rstd"][:, :], in_=st["var"][:, :]))
            W("scalar", [f])
            g = E("scalar", lambda s: s.activation(out=st["rstd"][:, :], in_=st["rstd"][:, :], func=AF.Sqrt))
            W("vector", [g])
            h = E("vector", lambda v: v.tensor_scalar(out=st["nbias"][:, :], in0=st["mean"][:, :], scalar1=st["rstd"][:, :], scalar2=-1.0, op0=OP.mult, op1=OP.mult))
            return g, h

        t_x1store = [None] * TT
        t_uload = [None] * TT
        trv_guard = [t_bst1, t_bst2]
        tra_guard = [t_zfd]
        n1b_guard = [t_bw2]
        xfmt_guard = [t_astk]
        ya_guard = [t_acc, t_cb1, t_cb2, t_acast]
        xb0_guard = [t_bbcast, t_bacc]
        xb1_guard = [t_rdim, t_cb1, t_cb2] + gb_toks
        last_convevac = None
        last_ptrb_evac = None
        last_axmm = None

        for t_i in range(TT):
            xbt = xb[t_i % 2]
            war = [t_x1store[t_i - 2]] if t_i >= 2 else ([t_ksum_done, t_bacc, t_bbcast] if t_i % 2 == 0 else [t_rdim, prev_ev])
            if t_i == 0:
                war += xb0_guard
            if t_i == 1:
                war += xb1_guard + [t_ksum_done]
            W("sync", war)
            t_uload[t_i] = DMA("sync", f"uin{t_i % 2}", lambda s, t_i=t_i, xbt=xbt: s.dma_start(
                out=xbt[:, :], in_=u[P * t_i:P * t_i + P, :]))

            W("vector", [t_uload[t_i]] + (trv_guard if t_i == 0 else []))
            t_rs = E("vector", lambda v, xbt=xbt: v.tensor_scalar(
                out=trv[:, :], in0=xbt[:, :], scalar1=1.0, scalar2=None, op0=OP.mult, op1=OP.add,
                accum_out=st1["s1"][:, :]))
            W("scalar", [t_uload[t_i]] + (tra_guard if t_i == 0 else []))
            t_sq = E("scalar", lambda s, xbt=xbt: s.activation(
                out=tra[:, :], in_=xbt[:, :], func=AF.Square, accum_out=st1["q1"][:, :]))
            t_r1, t_nb1 = ln_finalize(st1, D, t_rs, t_sq)
            W("scalar", [t_nb1, pr.now("tensor")] + ([t_ldbs] if t_i == 0 else []) + (n1b_guard if t_i == 0 else []))
            t_n1 = E("scalar", lambda s, xbt=xbt: s.activation(
                out=n1b[:, :], in_=xbt[:, :], func=AF.Identity, bias=st1["nbias"][:, :], scale=st1["rstd"][:, :]))

            ev_toks = []
            for r in range(4):
                toks = [t_n1]
                if r >= 2:
                    toks.append(ev_toks[r - 2])
                elif last_ptrb_evac is not None:
                    toks.append(last_ptrb_evac)
                if t_i == 0 and r < 2:
                    toks += [t_astk, t_kmbT]
                    toks += xfmt_guard
                W("tensor", toks)
                tlast = None
                for j in range(8):
                    c = 8 * r + j
                    tlast = E("tensor", lambda t, c=c, r=r, j=j: t.transpose(
                        out=ptrb[r % 2][:, 128 * j:128 * j + 128], in_=n1b[:, 128 * c:128 * c + 128],
                        identity=idsb[:, :]))
                W("vector", [tlast])
                ev_toks.append(E("vector", lambda v, r=r: v.tensor_copy(
                    out=xfmt[:, 1024 * r:1024 * r + 1024], in_=ptrb[r % 2][:, :])))
            last_ptrb_evac = ev_toks[-1]

            conv_ev = [None] * 8
            for g in range(8):
                toks = [ev_toks[-1], t_tband]
                if g >= 3:
                    toks.append(conv_ev[g - 3])
                elif last_convevac is not None:
                    toks.append(last_convevac)
                elif t_i == 0:
                    toks += [t_ksum_done, gb_toks[-1]]
                W("tensor", toks)
                tg = None
                for ij in range(4 * g + 4):
                    j0 = 512 * g - 128 * ij + 384
                    tg = E("tensor", lambda t, g=g, ij=ij, j0=j0: t.matmul(
                        out=pconv[g % 3][:, :], lhsT=xfmt[:, 128 * ij:128 * ij + 128],
                        rhs=tband[:, j0:j0 + 512], start=(ij == 0), stop=(ij == 4 * g + 3)))
                W("vector", [tg] + ([pr.now("scalar")] + (ya_guard if t_i == 0 else []) if g == 0 else []))
                conv_ev[g] = E("vector", lambda v, g=g: v.scalar_tensor_tensor(
                    out=ya[:, 512 * g:512 * g + 512], in0=n1b[:, 512 * g:512 * g + 512],
                    scalar=dbc[:, :], op0=OP.mult, in1=pconv[g % 3][:, :], op1=OP.add,
                    accum_out=yacols[:, g:g + 1]))
                W("scalar", [conv_ev[g]])
                E("scalar", lambda s, g=g: s.activation(
                    out=tra[:, 512 * g:512 * g + 512], in_=ya[:, 512 * g:512 * g + 512],
                    func=AF.Square, accum_out=yaqcols[:, g:g + 1]))
            last_convevac = conv_ev[7]
            t_sqa = pr.now("scalar")

            W("vector", [conv_ev[7]])
            t_ys = E("vector", lambda v: v.tensor_scalar(
                out=trv[:, 0:8], in0=yacols[:, :], scalar1=1.0, scalar2=None, op0=OP.mult, op1=OP.add,
                accum_out=sta["s1"][:, :]))
            W("vector", [t_sqa])
            t_yq = E("vector", lambda v: v.tensor_scalar(
                out=trv[:, 8:16], in0=yaqcols[:, :], scalar1=1.0, scalar2=None, op0=OP.mult, op1=OP.add,
                accum_out=sta["q1"][:, :]))
            t_ra, t_nba = ln_finalize(sta, D, t_ys, t_yq)
            W("vector", [t_ra, t_nba])
            t_nsv = E("vector", lambda v, t_i=t_i: v.tensor_copy(out=nall[:, t_i:t_i + 1], in_=sta["nbias"][:, :]))
            t_x1 = E("vector", lambda v, xbt=xbt: v.scalar_tensor_tensor(
                out=xbt[:, :], in0=ya[:, :], scalar=sta["rstd"][:, :], op0=OP.mult,
                in1=xbt[:, :], op1=OP.add))
            W("sync", [t_x1])
            t_x1store[t_i] = DMA("sync", f"x1o{t_i % 2}", lambda s, t_i=t_i, xbt=xbt: s.dma_start(
                out=x1_scr[P * t_i:P * t_i + P, :], in_=xbt[:, :]))

            W("vector", [t_x1])
            t_rs2 = E("vector", lambda v, xbt=xbt: v.tensor_scalar(
                out=trv[:, :], in0=xbt[:, :], scalar1=1.0, scalar2=None, op0=OP.mult, op1=OP.add,
                accum_out=st2["s1"][:, :]))
            W("scalar", [t_x1])
            t_sq2 = E("scalar", lambda s, xbt=xbt: s.activation(
                out=tra[:, :], in_=xbt[:, :], func=AF.Square, accum_out=st2["q1"][:, :]))
            t_r2, t_nb2 = ln_finalize(st2, D, t_rs2, t_sq2)
            W("scalar", [t_nb2, conv_ev[7]])
            t_n2 = E("scalar", lambda s, xbt=xbt: s.activation(
                out=n1b[:, :], in_=xbt[:, :], func=AF.Identity, bias=st2["nbias"][:, :], scale=st2["rstd"][:, :]))

            for r in range(4):
                toks = [t_n2, ev_toks[(r + 2) % 4]]
                W("tensor", toks)
                tlast = None
                for j in range(8):
                    c = 8 * r + j
                    tlast = E("tensor", lambda t, c=c, r=r, j=j: t.transpose(
                        out=ptrb[r % 2][:, 128 * j:128 * j + 128], in_=n1b[:, 128 * c:128 * c + 128],
                        identity=idsb[:, :]))
                W("vector", [tlast])
                t_ev2 = E("vector", lambda v, r=r, t_i=t_i: v.tensor_copy(
                    out=bass.AP(n2fm, TOK * 8 * r + P * t_i, [[FC * TOK, P], [TOK, 8], [1, P]]),
                    in_=ptrb[r % 2][:, :]))
                ev_toks[r] = t_ev2
                W("tensor", [t_ev2, t_astk])
                for j in range(8):
                    c = 8 * r + j
                    last_axmm = E("tensor", lambda t, c=c, t_i=t_i: t.matmul(
                        out=pax[:, P * t_i:P * t_i + P], lhsT=astkT[:, NS * c:NS * c + NS],
                        rhs=bass.AP(n2fm, TOK * c + P * t_i, [[FC * TOK, P], [1, P]]),
                        start=(c == 0), stop=(c == FC - 1)))
            last_ptrb_evac = ev_toks[3]

        W("vector", [last_axmm])
        t_axb = E("vector", lambda v: v.tensor_copy(out=axb[:, :], in_=pax[:, :]))

        for cm in reversed(ps_cms):
            cm.__exit__(None, None, None)
        ps_cms2 = []

        def psum_open2(name, shape, dt):
            cm = nc.psum_tensor(name, shape, dt)
            ps_cms2.append(cm)
            return cm.__enter__()

        pffn = [psum_open2(f"pffn{i}", [P, 512], F32) for i in range(8)]
        ps1_clear = [t_axb, last_convevac, last_ptrb_evac, t_kmbT]

        t_wload = {}
        t_fev = {}
        t_fsq = {}
        t_fst = {}
        t_ffmm = {}
        t_p1v = pr.now("vector")
        t_p1a = pr.now("scalar")
        t_yst = [None] * TT
        kglob = 0
        for oc in range(8):
            W("tensor", [t_bhl, t_axb, tc["ones2b"]] + ps1_clear)
            for t2 in range(TT):
                if oc > 0:
                    W("tensor", [t_fev[(oc - 1, t2)]])
                E("tensor", lambda t, t2=t2, oc=oc: t.matmul(
                    out=pffn[t2][:, :], lhsT=ones2bs[:, :], rhs=bhl[:, 512 * oc:512 * oc + 512],
                    start=True, stop=False))
            for f in range(FC):
                kg = kglob
                war = [t_ffmm[kg - 4]] if kg >= 4 else []
                W("gpsimd", war)
                t_wload[kg] = DMA("gpsimd", f"wload{kg % 4}", lambda g, f=f, oc=oc, kg=kg: g.dma_start(
                    out=wbuf[kg % 4][:, :], in_=wk[128 * f:128 * f + 128, 512 * oc:512 * oc + 512]))
                W("tensor", [t_wload[kg]])
                tlast = None
                for t2 in range(TT):
                    tlast = E("tensor", lambda t, f=f, t2=t2, kg=kg: t.matmul(
                        out=pffn[t2][:, :], lhsT=bass.AP(n2fm, TOK * f + P * t2, [[FC * TOK, P], [1, P]]),
                        rhs=wbuf[kg % 4][:, :], start=False, stop=False))
                t_ffmm[kg] = tlast
                kglob += 1
            for t2 in range(TT):
                t_dl = E("tensor", lambda t, t2=t2, oc=oc: t.matmul(
                    out=pffn[t2][:, :], lhsT=axb[:, P * t2:P * t2 + P], rhs=bstkT[:, 512 * oc:512 * oc + 512],
                    start=False, stop=True))
                idx = oc * TT + t2
                sbuf_i = idx % 4
                war = []
                if idx >= 4:
                    rprev = (idx - 4) % TT
                    rec = pr.dsems[f"frow{rprev}"]
                    war = [(rec[0], rec[1]), t_fsq[idx - 4]]
                W("vector", [t_dl] + war)
                t_fev[(oc, t2)] = E("vector", lambda v, t2=t2, oc=oc, sbuf_i=sbuf_i: v.tensor_scalar(
                    out=fstage[sbuf_i][:, :], in0=pffn[t2][:, :], scalar1=1.0, scalar2=None,
                    op0=OP.mult, op1=OP.add, accum_out=fsums[t2][:, oc:oc + 1]))
                W("scalar", [t_fev[(oc, t2)]])
                t_fsq[idx] = E("scalar", lambda s, sbuf_i=sbuf_i, t2=t2, oc=oc: s.activation(
                    out=tra[:, 512 * sbuf_i:512 * sbuf_i + 512], in_=fstage[sbuf_i][:, :],
                    func=AF.Square, accum_out=fsqs[t2][:, oc:oc + 1]))
                W("scalar", [t_fev[(oc, t2)]])
                t_fst[idx] = DMA("scalar", f"frow{t2}", lambda s, t2=t2, oc=oc, sbuf_i=sbuf_i: s.dma_start(
                    out=f_scr[P * t2:P * t2 + P, 512 * oc:512 * oc + 512], in_=fstage[sbuf_i][:, :]))
                if oc == 7:
                    t_i = t2
                    war = [t_p1v, t_p1a, t_x1store[TT - 1], t_x1store[TT - 2]]
                    war += [t_fst[o2 * TT + t_i] for o2 in range(8)]
                    if t_i >= 1:
                        war.append(t_yst[t_i - 1])
                    W("sync", war)
                    t_x1l = DMA("sync", "x1in", lambda s, t_i=t_i: s.dma_start(
                        out=xb[0][:, :], in_=x1_scr[P * t_i:P * t_i + P, :]))
                    t_fl = DMA("sync", "fin", lambda s, t_i=t_i: s.dma_start(
                        out=xb[1][:, :], in_=f_scr[P * t_i:P * t_i + P, :]))
                    W("vector", [t_fev[(7, t_i)]])
                    t_fs = E("vector", lambda v, t_i=t_i: v.tensor_scalar(
                        out=trv[:, 0:8], in0=fsums[t_i][:, :], scalar1=1.0, scalar2=None, op0=OP.mult, op1=OP.add,
                        accum_out=stf["s1"][:, :]))
                    W("vector", [t_fsq[7 * TT + t_i]])
                    t_fq = E("vector", lambda v, t_i=t_i: v.tensor_scalar(
                        out=trv[:, 8:16], in0=fsqs[t_i][:, :], scalar1=1.0, scalar2=None, op0=OP.mult, op1=OP.add,
                        accum_out=stf["q1"][:, :]))
                    t_rf, t_nbf = ln_finalize(stf, D, t_fs, t_fq)
                    W("scalar", [t_nbf, t_fl])
                    t_lnf = E("scalar", lambda s: s.activation(
                        out=xb[1][:, :], in_=xb[1][:, :], func=AF.Identity, bias=stf["nbias"][:, :], scale=stf["rstd"][:, :]))
                    W("vector", [t_lnf, t_x1l] + ([t_yst[t_i - 1]] if t_i >= 1 else []))
                    t_add = E("vector", lambda v, t_i=t_i: v.scalar_tensor_tensor(
                        out=ya[:, :], in0=xb[0][:, :], scalar=nall[:, t_i:t_i + 1], op0=OP.add,
                        in1=xb[1][:, :], op1=OP.add))
                    W("scalar", [t_add])
                    t_yst[t_i] = DMA("scalar", "yout", lambda s, t_i=t_i: s.dma_start(
                        out=y_out[P * t_i:P * t_i + P, :], in_=ya[:, :]))
        W("sync", [t_yst[TT - 1]])

        for cm in reversed(ps_cms2):
            cm.__exit__(None, None, None)

        block = e(nc.Block())
        pr.build(block)

    return nc


def make_in_map(core, inputs, t_tiles=8):
    C = host_constants()
    TOK = t_tiles * P
    u = np.asarray(inputs["L0_attn_u"]).reshape(N_CORES * 1024, D)
    gate = np.asarray(inputs["L0_ffn_gate"])
    grow = gate[core]
    gateq = np.empty((P, 4), np.float32)
    for k in range(4):
        for p in range(P):
            gateq[p, k] = grow[4 * k + p // 32]
    gater = np.broadcast_to(grow[None, :], (P, NL)).astype(np.float32).copy()
    m = {
        "u": np.ascontiguousarray(u[core * 1024: core * 1024 + TOK]),
        "lam_ri": np.asarray(inputs["L0_attn_Lambda_ri"], np.float32),
        "p_ri": np.asarray(inputs["L0_attn_p_ri"], np.float32),
        "q_ri": np.asarray(inputs["L0_attn_q_ri"], np.float32),
        "vc_ri": np.asarray(inputs["L0_attn_Vc_ri"], np.float32).reshape(NS, NS * 2),
        "ct_ri": np.asarray(inputs["L0_attn_Ct_ri"], np.float32).reshape(NS, 2),
        "bp": np.asarray(inputs["L0_attn_Bp"], np.float32).reshape(1, NS),
        "dsc": np.asarray(inputs["L0_attn_D"], np.float32).reshape(1, 1),
        "lstep": np.asarray(inputs["L0_attn_log_step"], np.float32).reshape(1, 1),
        "wk": np.asarray(inputs["L0_ffn_kernel"], np.float32),
        "bias": np.asarray(inputs["L0_ffn_bias"], np.float32).reshape(1, D),
        "lora_a": np.asarray(inputs["L0_ffn_lora_A"], np.float32).reshape(NL * RANK, D),
        "lora_b": np.asarray(inputs["L0_ffn_lora_B"], np.float32).reshape(NL, D * RANK),
        "lora_as": np.asarray(inputs["L0_ffn_lora_A_shared"], np.float32),
        "lora_bs": np.asarray(inputs["L0_ffn_lora_B_shared"], np.float32).reshape(1, D * RANK),
        "gateq": gateq,
        "gater": gater,
    }
    for name, arr in C.items():
        m[name] = arr
    return m


def kernel(**inputs):
    nc = build_nc(8)
    in_maps = [make_in_map(c, inputs) for c in range(N_CORES)]
    res = run_bass_kernel_spmd(nc, in_maps, core_ids=list(range(N_CORES)))
    out = np.concatenate([np.asarray(res.results[c]["y"]) for c in range(N_CORES)], axis=0)
    return out.reshape(1, N_CORES * 1024, D).astype(np.float32)
